# version 37
# baseline (speedup 1.0000x reference)
import numpy as np
import ml_dtypes
import concourse.bass as bass
import concourse.mybir as mybir
from concourse.bass_utils import run_bass_kernel_spmd

F32 = mybir.dt.float32
BF16 = mybir.dt.bfloat16
AF = mybir.ActivationFunctionType
ALU = mybir.AluOpType
AX = mybir.AxisListType

T, D, H, DK, DV, KD, VD, FF = 2048, 1024, 4, 128, 256, 512, 1024, 4096
C = 128
NCH = T // C
NSUP = 4
CPS = NCH // NSUP
NSL = 4
FSL = FF // NSL
EPS = 1e-5

THS = [0.24419428096993973, 0.4639691338428854, 0.6861859295255306,
       0.9059607823984763, 1.0839784132255623]
DS2 = [0.1099667994624956 / 2, 0.1295082645824852 / 2, 0.14791871969049653 / 2,
       0.16458580116661264 / 2, 0.17907899372791491 / 2]
C0 = sum(DS2)

ENGS = ("pe", "dve", "act", "sync", "gp")


class Sched:

    DMA_ROT = {"sync": ["s_hd%d" % i for i in range(8)],
               "gp": ["s_wd%d" % i for i in range(4)],
               "out": ["s_ot%d" % i for i in range(4)]}
    SEM_LAT = 300.0

    def __init__(self):
        self.nodes = []
        self.wmap = {}
        self.rmap = {}
        self.ps_reader = {}
        self.final_raw = []

    def op(self, eng, fn, reads=(), writes=(), out_dma=False, gp_compute=False,
           cost=850.0):
        deps = set()
        soft = set()
        for k in reads:
            if k in self.wmap:
                deps.add(self.wmap[k])
            if isinstance(k, tuple) and k and k[0] == "ps":
                lr = self.ps_reader.get(k)
                if lr is not None and lr[1] != eng:
                    deps.add(lr[0])
        for k in writes:
            if k in self.wmap:
                d = self.wmap[k]
                (soft if self.nodes[d]["eng"] == eng and not self.nodes[d]["dma"]
                 else deps).add(d)
            for d in self.rmap.get(k, ()):
                (soft if self.nodes[d]["eng"] == eng and not self.nodes[d]["dma"]
                 else deps).add(d)
        soft -= deps
        nid = len(self.nodes)
        cost = float(getattr(fn, "cost", cost))
        self.nodes.append(dict(eng=eng, fn=fn, deps=deps, soft=soft, cost=cost,
                               out_dma=out_dma,
                               dma=(eng in ("sync", "gp") and not gp_compute)))
        for k in writes:
            self.wmap[k] = nid
            self.rmap[k] = []
        for k in reads:
            self.rmap.setdefault(k, []).append(nid)
            if isinstance(k, tuple) and k and k[0] == "ps":
                self.ps_reader[k] = (nid, eng)
        for k in writes:
            self.ps_reader.pop(k, None)
        return nid

    def raw(self, eng, kind, *args):
        self.final_raw.append((eng, kind) + args)

    def schedule(self, reorder=True):
        n = len(self.nodes)
        order = {e: [] for e in ENGS}
        if not reorder:
            for i, nd in enumerate(self.nodes):
                order[nd["eng"]].append(i)
            return order
        import heapq
        ndep = [len(nd["deps"]) + len(nd["soft"]) for nd in self.nodes]
        kids = [[] for _ in range(n)]
        for i, nd in enumerate(self.nodes):
            for d in nd["deps"] | nd["soft"]:
                kids[d].append(i)
        ready = {e: [] for e in ENGS}
        rtime = [0.0] * n
        for i, nd in enumerate(self.nodes):
            if ndep[i] == 0:
                heapq.heappush(ready[nd["eng"]], i)
        eng_t = {e: 0.0 for e in ENGS}
        done_t = [0.0] * n
        scheduled = 0
        evq = []
        while scheduled < n:
            progressed = False
            for e in ENGS:
                if ready[e]:
                    i = heapq.heappop(ready[e])
                    nd = self.nodes[i]
                    st = max(eng_t[e], rtime[i])
                    ft = st + nd["cost"]
                    eng_t[e] = st + (200.0 if nd["dma"] else nd["cost"])
                    done_t[i] = ft
                    order[e].append(i)
                    heapq.heappush(evq, (ft, i))
                    scheduled += 1
                    progressed = True
            if not progressed:
                if not evq:
                    raise RuntimeError("scheduler stuck")
                ft, i = heapq.heappop(evq)
                for k in kids[i]:
                    ndep[k] -= 1
                    rtime[k] = max(rtime[k], ft + self.SEM_LAT)
                    if ndep[k] == 0:
                        heapq.heappush(ready[self.nodes[k]["eng"]], k)
                for e in ENGS:
                    eng_t[e] = max(eng_t[e], min(eng_t[e] + 0, ft))
        return order

    def emit(self, nc, sems, reorder=True):
        import os as _os
        use_drains = (_os.environ.get("BASS_SIM_FP64") == "1"
                      or _os.environ.get("K_SIM") == "1")
        order = self.schedule(reorder)
        pos = {}
        for e in ENGS:
            for p, i in enumerate(order[e]):
                pos[i] = (e, p)
        needs_inc = set()
        for i, nd in enumerate(self.nodes):
            if nd["dma"]:
                needs_inc.add(i)
            for d in nd["deps"]:
                needs_inc.add(d)
            if use_drains:
                for d in nd["soft"]:
                    needs_inc.add(d)
        info = {}
        cnt = {}
        rot_i = {"sync": 0, "gp": 0, "out": 0}
        for e in ENGS:
            for i in order[e]:
                nd = self.nodes[i]
                if nd["dma"]:
                    rk = "out" if nd["out_dma"] else e
                    sem = self.DMA_ROT[rk][rot_i[rk] % len(self.DMA_ROT[rk])]
                    rot_i[rk] += 1
                    inc = 16
                else:
                    sem = {"pe": "s_pe", "dve": "s_dve", "act": "s_act",
                           "gp": "s_gp", "sync": "s_sy"}[e]
                    inc = 1 if i in needs_inc else 0
                val = cnt.get(sem, 0) + inc
                cnt[sem] = val
                info[i] = (sem, val, inc)
        prog = {e: [] for e in ENGS}
        waited = {e: {} for e in ENGS}
        drain_upto = {e: -1 for e in ENGS}
        prev_dma_sem = {}
        for e in ENGS:
            for p, i in enumerate(order[e]):
                nd = self.nodes[i]
                sem, val, inc = info[i]
                deps = list(nd["deps"])
                if use_drains:
                    deps += list(nd["soft"])
                if nd["dma"]:
                    if sem in prev_dma_sem:
                        deps.append(prev_dma_sem[sem])
                    prev_dma_sem[sem] = i
                for d in deps:
                    dsem, dval, dinc = info[d]
                    if dval > waited[e].get(dsem, 0):
                        prog[e].append(("wait", dsem, dval))
                        waited[e][dsem] = dval
                prog[e].append(("call", nd["fn"], sem, inc))
        for item in self.final_raw:
            e, kind = item[0], item[1]
            if kind == "wait":
                s = item[2]
                v = cnt.get(s, item[3])
                prog[e].append(("wait", s, v))
        return prog


def build_nc():
    nc = bass.Bass("TRN2", target_bir_lowering=False)

    xd = nc.dram_tensor("x", [T, D], F32, kind="ExternalInput").ap()
    wq_d = nc.dram_tensor("wq", [D, KD], F32, kind="ExternalInput").ap()
    wk_d = nc.dram_tensor("wk", [D, KD], F32, kind="ExternalInput").ap()
    wv_d = nc.dram_tensor("wv", [D, VD], F32, kind="ExternalInput").ap()
    wg_d = nc.dram_tensor("wg", [D, VD], F32, kind="ExternalInput").ap()
    wgk1_d = nc.dram_tensor("wgk1", [D, 16], F32, kind="ExternalInput").ap()
    wgk2_d = nc.dram_tensor("wgk2", [16, KD], F32, kind="ExternalInput").ap()
    bgk2_d = nc.dram_tensor("bgk2", [128, 4], F32, kind="ExternalInput").ap()
    wout_d = nc.dram_tensor("wout", [VD, D], F32, kind="ExternalInput").ap()
    rmsw_d = nc.dram_tensor("rmsw", [DV], F32, kind="ExternalInput").ap()
    wfcT_d = nc.dram_tensor("wfcT", [D, FF], F32, kind="ExternalInput").ap()
    wprT_d = nc.dram_tensor("wprT", [FF, D], F32, kind="ExternalInput").ap()
    ident_d = nc.dram_tensor("ident", [128, 128], BF16, kind="ExternalInput").ap()
    tri_d = nc.dram_tensor("tri", [128, 128], BF16, kind="ExternalInput").ap()
    outd = nc.dram_tensor("out", [T, D], F32, kind="ExternalOutput").ap()

    from contextlib import ExitStack
    es = ExitStack()
    sb = lambda name, shape, dt: es.enter_context(nc.sbuf_tensor(name, shape, dt)).ap()

    wq = sb("wq_s", [128, 8 * KD], BF16)
    wk = sb("wk_s", [128, 8 * KD], BF16)
    wv = sb("wv_s", [128, 8 * VD], BF16)
    wg = sb("wg_s", [128, 8 * VD], BF16)
    wout = sb("wout_s", [128, 8 * D], BF16)
    wgk1 = sb("wgk1_s", [128, 8 * 16], BF16)
    wgk2 = sb("wgk2_s", [16, KD], BF16)
    wfc2 = [sb("wfc_s0", [128, 8 * 512], BF16), sb("wfc_s1", [128, 8 * 512], BF16)]
    wpr2 = [sb("wpr_s0", [128, 4 * D], BF16), sb("wpr_s1", [128, 4 * D], BF16)]
    ident = sb("ident_s", [128, 128], BF16)
    tri = sb("tri_s", [128, 128], BF16)
    rmsw = sb("rmsw_s", [128, VD], BF16)
    nbgk2 = sb("nbgk2_s", [128, 4], F32)
    eps_col = sb("eps_col", [128, 1], F32)
    one_col = sb("one_col", [128, 1], F32)
    th_col = sb("th_col", [128, 5], F32)
    x2 = [sb("x_sb0", [128, D], F32), sb("x_sb1", [128, D], F32)]
    xn_tm = sb("xn_tm", [128, D], BF16)
    xn_f2 = [sb("xn_fa", [128, 8 * (CPS * C)], BF16),
             sb("xn_fb", [128, 8 * (CPS * C)], BF16)]
    y2 = [sb("y_sba", [128, CPS * D], F32), sb("y_sbb", [128, CPS * D], F32)]
    h_sb = sb("h_sb", [128, 8 * 512], BF16)
    rl2 = [sb("rl0", [128, 512], BF16), sb("rl1", [128, 512], BF16)]
    u_bf = sb("u_bf", [16, 2 * C], BF16)
    ex_sb = sb("ex_sb", [128, KD], BF16)
    sp_sb = sb("sp_sb", [128, KD], F32)
    c_sb = sb("c_sb", [128, KD], F32)
    ct_sb = sb("ct_sb", [128, 4], F32)
    dcy2 = [sb("dcy0", [128, 4], F32), sb("dcy1", [128, 4], F32)]
    er = sb("er", [128, KD], BF16)
    qt2 = [sb("qt0", [128, KD], BF16), sb("qt1", [128, KD], BF16)]
    kt2 = [sb("kt0", [128, KD], BF16), sb("kt1", [128, KD], BF16)]
    kh2 = [sb("kh0", [128, KD], BF16), sb("kh1", [128, KD], BF16)]
    khT = sb("khT", [128, KD], BF16)
    v_bf = sb("v_bf", [128, VD], BF16)
    xwg2 = [sb("xwg0", [128, VD], BF16), sb("xwg1", [128, VD], BF16)]
    o_bf = sb("o_bf", [128, VD], BF16)
    S_bf = sb("S_bf", [128, H * DV], BF16)
    msq = sb("msq", [128, 4], F32)
    rfac = sb("rfac", [128, 4], F32)
    ev = sb("ev", [128, 256], F32)
    sl = sb("sl", [128, 256], F32)
    sg = sb("sg", [128, VD], BF16)
    og = sb("og", [128, VD], BF16)
    ogf = sb("ogf", [128, 8 * 128], BF16)
    sj2 = [sb("sj0", [128, D], BF16)]
    g2 = [sb("g_bf0", [128, D], BF16), sb("g_bf1", [128, D], BF16)]
    ob = sb("ob", [128, 512], F32)
    st_sum = sb("st_sum", [128, 1], F32)
    st_sa = sb("st_sa", [128, 1], F32)
    st_sb_ = sb("st_sb", [128, 1], F32)
    st_mean = sb("st_mean", [128, 1], F32)
    st_var = sb("st_var", [128, 1], F32)
    st_m2 = sb("st_m2", [128, 1], F32)
    st_rstd = sb("st_rstd", [128, 1], F32)
    st_nb = sb("st_nb", [128, 1], F32)

    ps = es.enter_context(nc.psum_tensor("ps", [128, 4096], F32)).ap()
    B = lambda b, o, n: ps[:, b * 512 + o: b * 512 + o + n]

    sems = {}
    semnames = (["s_pe", "s_dve", "s_act", "s_gp", "s_sy"] + Sched.DMA_ROT["sync"]
                + Sched.DMA_ROT["gp"] + Sched.DMA_ROT["out"])
    for s in semnames:
        sems[s] = es.enter_context(nc.semaphore(s))

    S = Sched()
    def dma(dst, src):
        f = lambda e: e.dma_start(dst, src)
        try:
            nb = src.nbytes()
        except Exception:
            nb = 1 << 20
        f.cost = 1500.0 + nb / 400.0
        return f

    def wsrc(d, rows, cols):
        return bass.AP(d.tensor, 0, [[cols, 128], [cols * 128, rows // 128], [1, cols]])
    S.op("gp", dma(wq.rearrange("p (c f) -> p c f", c=8), wsrc(wq_d, D, KD)), writes=[("wq",)])
    S.op("gp", dma(wk.rearrange("p (c f) -> p c f", c=8), wsrc(wk_d, D, KD)), writes=[("wk",)])
    S.op("gp", dma(wv.rearrange("p (c f) -> p c f", c=8), wsrc(wv_d, D, VD)), writes=[("wv",)])
    S.op("gp", dma(wg.rearrange("p (c f) -> p c f", c=8), wsrc(wg_d, D, VD)), writes=[("wg",)])
    S.op("gp", dma(wout.rearrange("p (c f) -> p c f", c=8), wsrc(wout_d, VD, D)), writes=[("wout",)])
    S.op("gp", dma(wgk1.rearrange("p (c f) -> p c f", c=8), wsrc(wgk1_d, D, 16)), writes=[("wgk1",)])
    S.op("gp", dma(wgk2, wgk2_d), writes=[("wgk2",)])
    S.op("gp", dma(rmsw.rearrange("p (h u) -> p h u", h=4),
                   bass.AP(rmsw_d.tensor, 0, [[0, 128], [0, 4], [1, DV]])), writes=[("rmsw",)])
    S.op("sync", dma(ident, ident_d), writes=[("ident",)])
    S.op("sync", dma(tri, tri_d), writes=[("tri",)])
    S.op("sync", dma(nbgk2, bgk2_d), writes=[("nbgk2",)])
    S.op("dve", lambda e: e.tensor_scalar_mul(nbgk2, nbgk2, -1.0), reads=[("nbgk2",)], writes=[("nbgk2",)])
    S.op("dve", lambda e: e.memset(S_bf, 0.0), writes=[("S",)])
    S.op("dve", lambda e: e.memset(eps_col, EPS), writes=[("eps",)])
    S.op("dve", lambda e: e.memset(one_col, 1.0), writes=[("one",)])
    for j in range(5):
        S.op("dve", lambda e, j_=j: e.memset(th_col[:, j_:j_ + 1], -THS[j_]), writes=[("th",)])

    def mmgroup(dst_slices, lhsTs, rhss, flags, warm=0):
        def f(e):
            last = None
            for d_, l_, r_, (st, sp) in zip(dst_slices, lhsTs, rhss, flags):
                last = e.matmul(d_, l_, r_, start=st, stop=sp)
            for _ in range(warm):
                last = e.ldweights(ident)
            return last
        f.cost = sum(max(d_.shape[-1] / 2.4, 100.0) + 15.0 for d_ in dst_slices) + warm * 107.0
        return f

    def chunk_xnf(sc, ci, dc, t0=0, n=C):
        base = dc * (CPS * C) + ci * C + t0
        return xn_f2[sc % 2][:, base: base + n]

    def emit_p1(sc, ci):
        ch = sc * CPS + ci
        t0 = ch * C
        x_sb = x2[ci % 2]
        xk = ("x", ci % 2)
        S.op("sync", dma(x_sb, xd[t0:t0 + C, :]), writes=[xk])
        S.op("dve", lambda e, x_=x_sb: e.tensor_reduce(st_sum, x_, AX.X, ALU.add),
             reads=[xk], writes=[("stsum",)])
        S.op("act", lambda e, x_=x_sb: e.activation(xn_tm[:, 0:512], x_[:, 0:512], AF.Square, accum_out=st_sa),
             reads=[xk], writes=[("xntm",), ("stsa",)])
        S.op("act", lambda e, x_=x_sb: e.activation(xn_tm[:, 512:1024], x_[:, 512:1024], AF.Square, accum_out=st_sb_),
             reads=[xk], writes=[("xntm",), ("stsb",)])
        S.op("dve", lambda e: e.tensor_scalar_mul(st_mean, st_sum, 1.0 / D),
             reads=[("stsum",)], writes=[("stmean",)])
        S.op("dve", lambda e: e.tensor_add(st_var, st_sa, st_sb_),
             reads=[("stsa",), ("stsb",)], writes=[("stvar",)])
        S.op("dve", lambda e: e.tensor_scalar(st_m2, st_mean, st_mean, None, ALU.mult),
             reads=[("stmean",)], writes=[("stm2",)])
        S.op("dve", lambda e: e.tensor_scalar(st_var, st_var, 1.0 / D, None, ALU.mult),
             reads=[("stvar",)], writes=[("stvar",)])
        S.op("dve", lambda e: e.tensor_sub(st_var, st_var, st_m2),
             reads=[("stvar",), ("stm2",)], writes=[("stvar",)])
        S.op("act", lambda e: e.activation(st_rstd, st_var, AF.Ln, bias=eps_col, scale=1.0),
             reads=[("stvar",), ("eps",)], writes=[("strstd",)])
        S.op("act", lambda e: e.activation(st_rstd, st_rstd, AF.Exp, bias=0.0, scale=-0.5),
             reads=[("strstd",)], writes=[("strstd",)])
        S.op("dve", lambda e: e.tensor_scalar(st_nb, st_mean, st_rstd, -1.0, ALU.mult, ALU.mult),
             reads=[("stmean",), ("strstd",)], writes=[("stnb",)])
        S.op("dve", lambda e, x_=x_sb: e.tensor_scalar(xn_tm, x_, st_rstd, st_nb, ALU.mult, ALU.add),
             reads=[xk, ("strstd",), ("stnb",)], writes=[("xntm",)])
        for w in range(2):
            S.op("pe", mmgroup(
                [B(0, dc2 * 128, 128) for dc2 in range(4)],
                [xn_tm[:, (w * 4 + dc2) * 128:(w * 4 + dc2 + 1) * 128] for dc2 in range(4)],
                [ident] * 4, [(True, True)] * 4),
                reads=[("xntm",), ("ident",)], writes=[("ps", 0)])
            if w == 0:
                S.op("act", lambda e, ci_=ci, sc_=sc: [e.activation(chunk_xnf(sc_, ci_, dc2),
                                                            B(0, dc2 * 128, 128), AF.Copy)
                                               for dc2 in range(4)][-1],
                     reads=[("ps", 0)], writes=[("xnfa", sc % 2, ci)])
            else:
                S.op("dve", lambda e, ci_=ci, sc_=sc: [e.tensor_copy(chunk_xnf(sc_, ci_, 4 + dc2),
                                                             B(0, dc2 * 128, 128))
                                               for dc2 in range(4)][-1],
                     reads=[("ps", 0)], writes=[("xnf", sc % 2, ci)])

    def emit_p2(sc):
        for sli in range(NSL):
            f0 = sli * FSL
            for hh in range(2):
                srch = bass.AP(wfcT_d.tensor, f0 + hh * 512,
                               [[FF, 128], [FF * 128, 8], [1, 512]])
                S.op("gp", dma(wfc2[hh].rearrange("p (c f) -> p c f", c=8), srch),
                     writes=[("wfc", hh)])
                srcp = bass.AP(wprT_d.tensor, (f0 + hh * 512) * D,
                               [[D, 128], [D * 128, 4], [1, D]])
                S.op("gp", dma(wpr2[hh].rearrange("p (c f) -> p c f", c=4), srcp),
                     writes=[("wpr", hh)])
            xk = ([("xnf", sc % 2, i) for i in range(4)]
                  + [("xnfa", sc % 2, i) for i in range(4)])
            y_sb = y2[sc % 2]
            for fc in range(8):
                hh = fc // 4
                fcl = fc % 4
                if sli == NSL - 1:
                    mb = 7
                else:
                    mb = 0 if fc % 2 == 0 else 7
                S.op("pe", mmgroup(
                    [B(mb, 0, 512)] * 8,
                    [wfc2[hh][:, dc * 512 + fcl * 128: dc * 512 + (fcl + 1) * 128]
                     for dc in range(8)],
                    [chunk_xnf(sc, 0, dc, 0, 512) for dc in range(8)],
                    [(dc == 0, dc == 7) for dc in range(8)]),
                    reads=[("wfc", hh)] + xk, writes=[("ps", mb)])
                rl = rl2[fc % 2]
                S.op("act", lambda e, rl_=rl, mb_=mb: e.activation(rl_, B(mb_, 0, 512), AF.Relu),
                     reads=[("ps", mb)], writes=[("rl", fc % 2)])
                S.op("gp", lambda e, fc_=fc, rl_=rl: e.tensor_mul(
                    h_sb[:, fc_ * 512:(fc_ + 1) * 512], rl_, rl_),
                    reads=[("rl", fc % 2)], writes=[("h", fc)], gp_compute=True)
            for tt_ in range(4):
                j = tt_
                yb = 1 if tt_ % 2 == 0 else 3
                for fc in range(8):
                    hh = fc // 4
                    fcl = fc % 4
                    dsts, ls, rs, fl = [], [], [], []
                    for hf in range(2):
                        dsts.append(ps[:, yb * 512 + hf * 512: yb * 512 + hf * 512 + 512])
                        ls.append(h_sb[:, fc * 512 + tt_ * 128: fc * 512 + (tt_ + 1) * 128])
                        rs.append(wpr2[hh][:, fcl * D + hf * 512: fcl * D + (hf + 1) * 512])
                        fl.append((fc == 0, fc == 7))
                    S.op("pe", mmgroup(dsts, ls, rs, fl),
                         reads=[("wpr", hh), ("h", fc)],
                         writes=[("ps", yb), ("ps", yb + 1)])
                j_ = j
                if sli == 0:
                    S.op("dve", lambda e, j_=j_, yb_=yb, y_=y_sb: e.tensor_copy(
                        y_[:, j_ * D:(j_ + 1) * D], ps[:, yb_ * 512: yb_ * 512 + 1024]),
                        reads=[("ps", yb), ("ps", yb + 1)],
                        writes=[("y", sc % 2, j, 0), ("y", sc % 2, j, 1)])
                else:
                    S.op("dve", lambda e, j_=j_, yb_=yb, y_=y_sb: e.tensor_tensor(
                        y_[:, j_ * D:(j_ + 1) * D], ps[:, yb_ * 512: yb_ * 512 + 1024],
                        y_[:, j_ * D:(j_ + 1) * D], ALU.add),
                        reads=[("ps", yb), ("ps", yb + 1), ("y", sc % 2, j, 0), ("y", sc % 2, j, 1)],
                        writes=[("y", sc % 2, j, 0), ("y", sc % 2, j, 1)])

    def p3_qk(sc, p):
        ci = 2 * p
        xk = [("xnf", sc % 2, ci), ("xnf", sc % 2, ci + 1),
              ("xnfa", sc % 2, ci), ("xnfa", sc % 2, ci + 1)]
        for name, wmat, bq in (("q", wq, 1), ("k", wk, 3)):
            dsts, ls, rs, fl = [], [], [], []
            for kc in range(4):
                for dc in range(8):
                    dsts.append(ps[:, bq * 512 + kc * 256: bq * 512 + kc * 256 + 256])
                    ls.append(wmat[:, dc * KD + kc * 128: dc * KD + (kc + 1) * 128])
                    rs.append(chunk_xnf(sc, ci, dc, 0, 256))
                    fl.append((dc == 0, dc == 7))
            S.op("pe", mmgroup(dsts, ls, rs, fl),
                 reads=[("wq" if name == "q" else "wk",)] + xk,
                 writes=[("ps", bq), ("ps", bq + 1)])

    def p3_u(sc, p):
        ci = 2 * p
        xk = [("xnf", sc % 2, ci), ("xnf", sc % 2, ci + 1),
              ("xnfa", sc % 2, ci), ("xnfa", sc % 2, ci + 1)]
        S.op("pe", mmgroup(
            [ps[0:16, 0:256]] * 8,
            [wgk1[:, dc * 16:(dc + 1) * 16] for dc in range(8)],
            [chunk_xnf(sc, ci, dc, 0, 256) for dc in range(8)],
            [(dc == 0, dc == 7) for dc in range(8)]),
            reads=[("wgk1",)] + xk, writes=[("ps", 0)])
        S.op("dve", lambda e: e.tensor_copy(u_bf, ps[0:16, 0:256]),
             reads=[("ps", 0)], writes=[("u", 0), ("u", 1)])

    def p3_gates_mm(sc, ci):
        xk = [("xnf", sc % 2, ci), ("xnfa", sc % 2, ci)]
        qt, kt, kh, dcy = qt2[ci % 2], kt2[ci % 2], kh2[ci % 2], dcy2[ci % 2]
        hfu = (ci % 2) * 128
        S.op("pe", mmgroup(
            [B(0, kc * 128, 128) for kc in range(4)],
            [wgk2[:, kc * 128:(kc + 1) * 128] for kc in range(4)],
            [u_bf[:, hfu:hfu + 128]] * 4, [(True, True)] * 4, warm=8),
            reads=[("wgk2",), ("u", ci % 2)], writes=[("ps", 0)])
        S.op("act", lambda e: [e.activation(ex_sb[:, kc * 128:(kc + 1) * 128],
                                            B(0, kc * 128, 128), AF.Exp,
                                            bias=nbgk2[:, kc:kc + 1], scale=-1.0)
                               for kc in range(4)][-1],
             reads=[("ps", 0), ("nbgk2",)], writes=[("exa",)])
        S.op("act", lambda e: e.activation(sp_sb, ex_sb, AF.Ln, bias=one_col, scale=1.0),
             reads=[("exa",), ("one",)], writes=[("sp",)])
        S.op("dve", lambda e: e.tensor_scalar(sp_sb, sp_sb, 1.0 / 16.0, 3.0, ALU.mult, ALU.min),
             reads=[("sp",)], writes=[("sp",)])
        S.op("dve", lambda e: [e.tensor_tensor_scan(c_sb[:, kc * 128:(kc + 1) * 128],
                                                    sp_sb[:, kc * 128:(kc + 1) * 128],
                                                    sp_sb[:, kc * 128:(kc + 1) * 128],
                                                    0.0, ALU.add, ALU.bypass)
                               for kc in range(4)][-1],
             reads=[("sp",)], writes=[("c",)])
        ctv = bass.AP(c_sb.tensor, 127, [[c_sb.ap[0][0], 128], [128, 4]])
        S.op("dve", lambda e, ctv_=ctv: e.tensor_copy(ct_sb, ctv_), reads=[("c",)], writes=[("ct",)])
        S.op("act", lambda e: e.activation(ex_sb, c_sb, AF.Exp, bias=0.0, scale=-1.0),
             reads=[("c",)], writes=[("exa",)])
        S.op("act", lambda e: e.activation(er, c_sb, AF.Exp, bias=0.0, scale=1.0),
             reads=[("c",)], writes=[("er",)])
        S.op("act", lambda e, dcy_=dcy: e.activation(dcy_, ct_sb, AF.Exp, bias=0.0, scale=-1.0),
             reads=[("ct",)], writes=[("dcy", ci % 2)])

    def p3_gates_post(sc, ci):
        qt, kt, kh = qt2[ci % 2], kt2[ci % 2], kh2[ci % 2]
        hf = (ci % 2) * 128
        S.op("dve", lambda e, hf_=hf, qt_=qt: [e.tensor_tensor(
            qt_[:, kc * 128:(kc + 1) * 128],
            ps[:, 512 + kc * 256 + hf_: 512 + kc * 256 + hf_ + 128],
            ex_sb[:, kc * 128:(kc + 1) * 128], ALU.mult) for kc in range(4)][-1],
            reads=[("ps", 1), ("ps", 2), ("exa",)], writes=[("qt", ci % 2)])
        S.op("dve", lambda e, hf_=hf, kt_=kt: [e.tensor_tensor(
            kt_[:, kc * 128:(kc + 1) * 128],
            ps[:, 3 * 512 + kc * 256 + hf_: 3 * 512 + kc * 256 + hf_ + 128],
            er[:, kc * 128:(kc + 1) * 128], ALU.mult) for kc in range(4)][-1],
            reads=[("ps", 3), ("ps", 4), ("er",)], writes=[("kt", ci % 2)])
        dcy_ = dcy2[ci % 2]
        S.op("dve", lambda e, kh_=kh, kt_=kt, d_=dcy_: [e.tensor_scalar(
            kh_[:, kc * 128:(kc + 1) * 128], kt_[:, kc * 128:(kc + 1) * 128],
            d_[:, kc:kc + 1], None, ALU.mult) for kc in range(4)][-1],
            reads=[("kt", ci % 2), ("dcy", ci % 2)], writes=[("kh", ci % 2)])

    def p3_vw(sc, ci):
        xk = [("xnf", sc % 2, ci), ("xnfa", sc % 2, ci)]
        xwg_bf = xwg2[ci % 2]
        for name, wmat, buf, bkey in (("v", wv, v_bf, ("v",)), ("w", wg, xwg_bf, ("xwg", ci % 2))):
            dsts, ls, rs, fl = [], [], [], []
            for hf in range(2):
                for dc in range(8):
                    dsts.append(B(5, 0, 1024)[:, hf * 512:(hf + 1) * 512])
                    ls.append(chunk_xnf(sc, ci, dc))
                    rs.append(wmat[:, dc * VD + hf * 512: dc * VD + (hf + 1) * 512])
                    fl.append((dc == 0, dc == 7))
            S.op("pe", mmgroup(dsts, ls, rs, fl),
                 reads=[("wv" if name == "v" else "wg",)] + xk, writes=[("ps", 5), ("ps", 6)])
            if name == "v":
                S.op("dve", lambda e, buf_=buf: e.tensor_copy(buf_, B(5, 0, 1024)),
                     reads=[("ps", 5), ("ps", 6)], writes=[bkey])
            else:
                S.op("act", lambda e, buf_=buf: e.activation(buf_, B(5, 0, 1024), AF.Copy),
                     reads=[("ps", 5), ("ps", 6)], writes=[bkey])

    def p3_mid(sc, ci):
        qt, kt, kh, dcy = qt2[ci % 2], kt2[ci % 2], kh2[ci % 2], dcy2[ci % 2]
        pt = kt
        S.op("pe", mmgroup(
            [B(7, h * 128, 128) for h in range(H)],
            [kh[:, h * 128:(h + 1) * 128] for h in range(H)],
            [ident] * H, [(True, True)] * H, warm=8),
            reads=[("kh", ci % 2), ("ident",)], writes=[("ps", 7)])
        S.op("dve", lambda e: e.tensor_copy(khT, B(7, 0, 512)),
             reads=[("ps", 7)], writes=[("khT",)])
        S.op("pe", mmgroup(
            [B(7, h * 128, 128) for h in range(H)],
            [kt[:, h * 128:(h + 1) * 128] for h in range(H)],
            [qt[:, h * 128:(h + 1) * 128] for h in range(H)],
            [(True, True)] * H, warm=8),
            reads=[("kt", ci % 2), ("qt", ci % 2)], writes=[("ps", 7)])
        S.op("dve", lambda e: [e.tensor_tensor(pt[:, h * 128:(h + 1) * 128],
                                               B(7, h * 128, 128), tri, ALU.mult)
                               for h in range(H)][-1],
             reads=[("ps", 7), ("tri",)], writes=[("kt", ci % 2)])
        dsts, ls, rs, fl = [], [], [], []
        for h in range(H):
            dsts += [B(5, 0, 1024)[:, h * 256:(h + 1) * 256]] * 2
            ls += [pt[:, h * 128:(h + 1) * 128], qt[:, h * 128:(h + 1) * 128]]
            rs += [v_bf[:, h * 256:(h + 1) * 256], S_bf[:, h * 256:(h + 1) * 256]]
            fl += [(True, False), (False, True)]
        S.op("pe", mmgroup(dsts, ls, rs, fl, warm=8),
             reads=[("kt", ci % 2), ("qt", ci % 2), ("v",), ("S",)], writes=[("ps", 5), ("ps", 6)])
        S.op("dve", lambda e: e.tensor_copy(o_bf, B(5, 0, 1024)),
             reads=[("ps", 5), ("ps", 6)], writes=[("o",)])
        S.op("pe", mmgroup(
            [B(5, 0, 1024)[:, h * 256:(h + 1) * 256] for h in range(H)],
            [khT[:, h * 128:(h + 1) * 128] for h in range(H)],
            [v_bf[:, h * 256:(h + 1) * 256] for h in range(H)],
            [(True, True)] * H, warm=8),
            reads=[("khT",), ("v",)], writes=[("ps", 5), ("ps", 6)])
        S.op("dve", lambda e, dcy_=dcy: [e.scalar_tensor_tensor(S_bf[:, h * 256:(h + 1) * 256],
                                                      S_bf[:, h * 256:(h + 1) * 256],
                                                      dcy_[:, h:h + 1],
                                                      B(5, 0, 1024)[:, h * 256:(h + 1) * 256],
                                                      ALU.mult, ALU.add)
                               for h in range(H)][-1],
             reads=[("ps", 5), ("ps", 6), ("dcy", ci % 2), ("S",)], writes=[("S",)])

    def p3_silu(sc, ci):
        xwg_bf = xwg2[ci % 2]
        xwk = ("xwg", ci % 2)
        for hh in range(4):
            S.op("act", lambda e, hh_=hh, x_=xwg_bf: e.activation(
                ev, x_[:, hh_ * 256:(hh_ + 1) * 256], AF.Exp),
                reads=[xwk], writes=[("ev",)])
            S.op("act", lambda e: e.activation(sl, ev, AF.Ln, bias=one_col, scale=1.0),
                 reads=[("ev",), ("one",)], writes=[("sl",)])
            S.op("dve", lambda e, hh_=hh, x_=xwg_bf: e.tensor_sub(
                ev, x_[:, hh_ * 256:(hh_ + 1) * 256], sl),
                reads=[xwk, ("sl",)], writes=[("ev",)])
            S.op("act", lambda e, hh_=hh: e.activation(sg[:, hh_ * 256:(hh_ + 1) * 256],
                                                       ev, AF.Exp),
                 reads=[("ev",)], writes=[("sg", hh % 2)])

    def p3_epi_pre(sc, ci):
        xwg_bf = xwg2[ci % 2]
        xwk = ("xwg", ci % 2)
        S.op("act", lambda e: [e.activation(og[:, h * 256:(h + 1) * 256],
                                            o_bf[:, h * 256:(h + 1) * 256],
                                            AF.Square, accum_out=msq[:, h:h + 1])
                               for h in range(H)][-1],
             reads=[("o",)], writes=[("msq",), ("og",)])
        S.op("act", lambda e: e.activation(rfac, msq, AF.Ln, bias=eps_col, scale=1.0 / DV),
             reads=[("msq",), ("eps",)], writes=[("rfac",)])
        S.op("act", lambda e: e.activation(rfac, rfac, AF.Exp, bias=0.0, scale=-0.5),
             reads=[("rfac",)], writes=[("rfac",)])
        S.op("dve", lambda e: e.tensor_tensor(sg, sg, rmsw, ALU.mult),
             reads=[("sg", 0), ("sg", 1), ("rmsw",)], writes=[("sg", 0), ("sg", 1)])
        S.op("dve", lambda e, x_=xwg_bf: e.tensor_tensor(sg, sg, x_, ALU.mult),
             reads=[("sg", 0), ("sg", 1), xwk], writes=[("sg", 0), ("sg", 1)])
        S.op("dve", lambda e: [e.scalar_tensor_tensor(
            og[:, h * 256:(h + 1) * 256],
            o_bf[:, h * 256:(h + 1) * 256], rfac[:, h:h + 1],
            sg[:, h * 256:(h + 1) * 256], ALU.mult, ALU.mult)
            for h in range(H)][-1],
            reads=[("o",), ("rfac",), ("sg", 0), ("sg", 1)], writes=[("og",)])

    def p3_epi_ogt(sc, ci):
        for w in range(2):
            S.op("pe", mmgroup(
                [B(7, vc2 * 128, 128) for vc2 in range(4)],
                [og[:, (w * 4 + vc2) * 128:(w * 4 + vc2 + 1) * 128] for vc2 in range(4)],
                [ident] * 4, [(True, True)] * 4, warm=8),
                reads=[("og",), ("ident",)], writes=[("ps", 7)])
            if w == 0:
                S.op("act", lambda e: e.activation(ogf[:, 0:512], B(7, 0, 512), AF.Copy),
                     reads=[("ps", 7)], writes=[("ogfa",)])
            else:
                S.op("dve", lambda e: e.tensor_copy(ogf[:, 512:1024], B(7, 0, 512)),
                     reads=[("ps", 7)], writes=[("ogf",)])

    def p3_spike(sc, ci):
        yk = [("y", sc % 2, ci, 0), ("y", sc % 2, ci, 1)]
        yi = y2[sc % 2][:, ci * D:(ci + 1) * D]
        gb = g2[ci % 2]
        for j in range(5):
            sjb = sj2[0]
            S.op("act", lambda e, j_=j, s_=sjb, yi_=yi: e.activation(
                s_, yi_, AF.Sign, bias=th_col[:, j_:j_ + 1], scale=1.0),
                reads=yk + [("th",)], writes=[("sj", 0)])
            if j == 0:
                S.op("dve", lambda e, s_=sjb, gb_=gb: e.tensor_scalar(gb_, s_, DS2[0], C0, ALU.mult, ALU.add),
                     reads=[("sj", 0)], writes=[("g", ci % 2)])
            else:
                S.op("dve", lambda e, j_=j, s_=sjb, gb_=gb: e.scalar_tensor_tensor(
                    gb_, s_, DS2[j_], gb_, ALU.mult, ALU.add),
                    reads=[("sj", 0), ("g", ci % 2)], writes=[("g", ci % 2)])

    def p3_tail(sc, ci):
        ch = sc * CPS + ci
        t0 = ch * C
        bq = 1 if ci % 2 == 0 else 3
        for half in range(2):
            dsts, ls, rs, fl = [], [], [], []
            for hf in range(2):
                for vc in range(half * 4, half * 4 + 4):
                    dsts.append(ps[:, bq * 512 + hf * 512: bq * 512 + hf * 512 + 512])
                    ls.append(ogf[:, vc * 128:(vc + 1) * 128])
                    rs.append(wout[:, vc * D + hf * 512: vc * D + (hf + 1) * 512])
                    fl.append((vc == half * 4 and half == 0, vc == half * 4 + 3 and half == 1))
            S.op("pe", mmgroup(dsts, ls, rs, fl),
                 reads=[("ogfa",) if half == 0 else ("ogf",), ("wout",)],
                 writes=[("ps", bq), ("ps", bq + 1)])
        yslot = y2[sc % 2][:, ci * D:(ci + 1) * D]
        S.op("sync", dma(yslot, xd[t0:t0 + C, :]),
             writes=[("y", sc % 2, ci, 0), ("y", sc % 2, ci, 1)])
        gb = g2[ci % 2]
        for hh in range(2):
            yk1 = ("y", sc % 2, ci, hh)
            S.op("dve", lambda e, bq_=bq, gb_=gb, hh_=hh: e.scalar_tensor_tensor(
                ob, ps[:, bq_ * 512 + hh_ * 512: bq_ * 512 + hh_ * 512 + 512], 0.0,
                gb_[:, hh_ * 512:(hh_ + 1) * 512], ALU.bypass, ALU.mult),
                reads=[("ps", bq + hh), ("g", ci % 2)], writes=[("ob",)])
            S.op("dve", lambda e, y_=yslot, hh_=hh: e.tensor_tensor(
                y_[:, hh_ * 512:(hh_ + 1) * 512], ob,
                y_[:, hh_ * 512:(hh_ + 1) * 512], ALU.add),
                reads=[("ob",), yk1], writes=[yk1])
            S.op("sync", lambda e, t0_=t0, hh_=hh, y_=yslot: e.dma_start(
                outd[t0_:t0_ + C, hh_ * 512:(hh_ + 1) * 512], y_[:, hh_ * 512:(hh_ + 1) * 512]),
                reads=[yk1], writes=[("outdma", ch, hh)], out_dma=True)

    for sc in range(NSUP):
        for ci in range(CPS):
            emit_p1(sc, ci)
        emit_p2(sc)
        p3_u(sc, 0)
        p3_gates_mm(sc, 0)
        p3_vw(sc, 0)
        for p in range(CPS // 2):
            A, Bc = 2 * p, 2 * p + 1
            p3_qk(sc, p)
            p3_gates_post(sc, A)
            p3_gates_mm(sc, Bc)
            p3_mid(sc, A)
            p3_gates_post(sc, Bc)
            p3_silu(sc, A)
            p3_epi_pre(sc, A)
            p3_vw(sc, Bc)
            p3_spike(sc, A)
            p3_mid(sc, Bc)
            p3_silu(sc, Bc)
            p3_epi_ogt(sc, A)
            p3_epi_pre(sc, Bc)
            p3_tail(sc, A)
            if p < CPS // 2 - 1:
                p3_u(sc, p + 1)
                p3_gates_mm(sc, 2 * p + 2)
                p3_vw(sc, 2 * p + 2)
            p3_spike(sc, Bc)
            p3_epi_ogt(sc, Bc)
            p3_tail(sc, Bc)

    for s in Sched.DMA_ROT["out"]:
        S.raw("sync", "wait", s, 0)

    import os as _os
    prog = S.emit(nc, sems, reorder=_os.environ.get("K_NO_REORDER") != "1")
    with nc.Block() as block:
        def runner(eng_name):
            def run(e):
                for item in prog[eng_name]:
                    if item[0] == "wait":
                        e.wait_ge(sems[item[1]], item[2])
                    elif item[0] == "drain":
                        e.drain()
                    else:
                        _, fn, sem, inc = item
                        inst = fn(e)
                        if isinstance(inst, list):
                            inst = inst[-1]
                        if inc:
                            inst.then_inc(sems[sem], inc)
            return run
        block.tensor(runner("pe"))
        block.vector(runner("dve"))
        block.scalar(runner("act"))
        block.sync(runner("sync"))
        block.gpsimd(runner("gp"))

    es.close()
    return nc


_NC_CACHE = None


def kernel(L0_attn_x, L0_attn_Wq, L0_attn_Wk, L0_attn_Wv, L0_attn_Wg,
           L0_attn_Wgk1, L0_attn_Wgk2, L0_attn_bgk2, L0_attn_Wout,
           L0_attn_rms_w, L0_ffn_W_fc, L0_ffn_W_proj):
    global _NC_CACHE
    n = 8
    x = np.ascontiguousarray(np.asarray(L0_attn_x, dtype=np.float32))
    f32 = lambda a: np.ascontiguousarray(np.asarray(a, dtype=np.float32))
    com = {
        "wq": f32(L0_attn_Wq), "wk": f32(L0_attn_Wk), "wv": f32(L0_attn_Wv),
        "wg": f32(L0_attn_Wg), "wgk1": f32(L0_attn_Wgk1), "wgk2": f32(L0_attn_Wgk2),
        "bgk2": np.ascontiguousarray(f32(L0_attn_bgk2).reshape(4, 128).T),
        "wout": f32(L0_attn_Wout), "rmsw": f32(L0_attn_rms_w),
        "wfcT": np.ascontiguousarray(f32(L0_ffn_W_fc).T),
        "wprT": np.ascontiguousarray(f32(L0_ffn_W_proj).T),
        "ident": np.eye(128, dtype=ml_dtypes.bfloat16),
        "tri": np.triu(np.ones((128, 128))).astype(ml_dtypes.bfloat16),
    }
    if _NC_CACHE is None:
        _NC_CACHE = build_nc()
    nc = _NC_CACHE
    in_maps = [dict(com, x=np.ascontiguousarray(x[b])) for b in range(n)]
    res = run_bass_kernel_spmd(nc, in_maps, core_ids=list(range(n)))
    return np.stack([np.asarray(r["out"], dtype=np.float32) for r in res.results], axis=0)


# version 38
# speedup vs baseline: 1.0443x; 1.0443x over previous
import numpy as np
import ml_dtypes
import concourse.bass as bass
import concourse.mybir as mybir
from concourse.bass_utils import run_bass_kernel_spmd

F32 = mybir.dt.float32
BF16 = mybir.dt.bfloat16
AF = mybir.ActivationFunctionType
ALU = mybir.AluOpType
AX = mybir.AxisListType

T, D, H, DK, DV, KD, VD, FF = 2048, 1024, 4, 128, 256, 512, 1024, 4096
C = 128
NCH = T // C
NSUP = 4
CPS = NCH // NSUP
NSL = 4
FSL = FF // NSL
EPS = 1e-5

THS = [0.24419428096993973, 0.4639691338428854, 0.6861859295255306,
       0.9059607823984763, 1.0839784132255623]
DS2 = [0.1099667994624956 / 2, 0.1295082645824852 / 2, 0.14791871969049653 / 2,
       0.16458580116661264 / 2, 0.17907899372791491 / 2]
C0 = sum(DS2)

ENGS = ("pe", "dve", "act", "sync", "gp")


class Sched:

    DMA_ROT = {"sync": ["s_hd%d" % i for i in range(8)],
               "gp": ["s_wd%d" % i for i in range(4)],
               "out": ["s_ot%d" % i for i in range(4)]}
    SEM_LAT = 300.0

    def __init__(self):
        self.nodes = []
        self.wmap = {}
        self.rmap = {}
        self.ps_reader = {}
        self.final_raw = []

    def op(self, eng, fn, reads=(), writes=(), out_dma=False, gp_compute=False,
           cost=850.0):
        deps = set()
        soft = set()
        for k in reads:
            if k in self.wmap:
                deps.add(self.wmap[k])
            if isinstance(k, tuple) and k and k[0] == "ps":
                lr = self.ps_reader.get(k)
                if lr is not None and lr[1] != eng:
                    deps.add(lr[0])
        for k in writes:
            if k in self.wmap:
                d = self.wmap[k]
                (soft if self.nodes[d]["eng"] == eng and not self.nodes[d]["dma"]
                 else deps).add(d)
            for d in self.rmap.get(k, ()):
                (soft if self.nodes[d]["eng"] == eng and not self.nodes[d]["dma"]
                 else deps).add(d)
        soft -= deps
        nid = len(self.nodes)
        cost = float(getattr(fn, "cost", cost))
        self.nodes.append(dict(eng=eng, fn=fn, deps=deps, soft=soft, cost=cost,
                               out_dma=out_dma,
                               dma=(eng in ("sync", "gp") and not gp_compute)))
        for k in writes:
            self.wmap[k] = nid
            self.rmap[k] = []
        for k in reads:
            self.rmap.setdefault(k, []).append(nid)
            if isinstance(k, tuple) and k and k[0] == "ps":
                self.ps_reader[k] = (nid, eng)
        for k in writes:
            self.ps_reader.pop(k, None)
        return nid

    def raw(self, eng, kind, *args):
        self.final_raw.append((eng, kind) + args)

    def schedule(self, reorder=True):
        n = len(self.nodes)
        order = {e: [] for e in ENGS}
        if not reorder:
            for i, nd in enumerate(self.nodes):
                order[nd["eng"]].append(i)
            return order
        import heapq
        ndep = [len(nd["deps"]) + len(nd["soft"]) for nd in self.nodes]
        kids = [[] for _ in range(n)]
        for i, nd in enumerate(self.nodes):
            for d in nd["deps"] | nd["soft"]:
                kids[d].append(i)
        ready = {e: [] for e in ENGS}
        rtime = [0.0] * n
        for i, nd in enumerate(self.nodes):
            if ndep[i] == 0:
                heapq.heappush(ready[nd["eng"]], i)
        eng_t = {e: 0.0 for e in ENGS}
        done_t = [0.0] * n
        scheduled = 0
        evq = []
        while scheduled < n:
            progressed = False
            for e in ENGS:
                if ready[e]:
                    i = heapq.heappop(ready[e])
                    nd = self.nodes[i]
                    st = max(eng_t[e], rtime[i])
                    ft = st + nd["cost"]
                    eng_t[e] = st + (200.0 if nd["dma"] else nd["cost"])
                    done_t[i] = ft
                    order[e].append(i)
                    heapq.heappush(evq, (ft, i))
                    scheduled += 1
                    progressed = True
            if not progressed:
                if not evq:
                    raise RuntimeError("scheduler stuck")
                ft, i = heapq.heappop(evq)
                for k in kids[i]:
                    ndep[k] -= 1
                    rtime[k] = max(rtime[k], ft + self.SEM_LAT)
                    if ndep[k] == 0:
                        heapq.heappush(ready[self.nodes[k]["eng"]], k)
                for e in ENGS:
                    eng_t[e] = max(eng_t[e], min(eng_t[e] + 0, ft))
        return order

    def emit(self, nc, sems, reorder=True):
        import os as _os
        use_drains = (_os.environ.get("BASS_SIM_FP64") == "1"
                      or _os.environ.get("K_SIM") == "1")
        order = self.schedule(reorder)
        pos = {}
        for e in ENGS:
            for p, i in enumerate(order[e]):
                pos[i] = (e, p)
        needs_inc = set()
        for i, nd in enumerate(self.nodes):
            if nd["dma"]:
                needs_inc.add(i)
            for d in nd["deps"]:
                needs_inc.add(d)
            if use_drains:
                for d in nd["soft"]:
                    needs_inc.add(d)
        info = {}
        cnt = {}
        rot_i = {"sync": 0, "gp": 0, "out": 0}
        for e in ENGS:
            for i in order[e]:
                nd = self.nodes[i]
                if nd["dma"]:
                    rk = "out" if nd["out_dma"] else e
                    sem = self.DMA_ROT[rk][rot_i[rk] % len(self.DMA_ROT[rk])]
                    rot_i[rk] += 1
                    inc = 16
                else:
                    sem = {"pe": "s_pe", "dve": "s_dve", "act": "s_act",
                           "gp": "s_gp", "sync": "s_sy"}[e]
                    inc = 1 if i in needs_inc else 0
                val = cnt.get(sem, 0) + inc
                cnt[sem] = val
                info[i] = (sem, val, inc)
        prog = {e: [] for e in ENGS}
        waited = {e: {} for e in ENGS}
        drain_upto = {e: -1 for e in ENGS}
        prev_dma_sem = {}
        for e in ENGS:
            for p, i in enumerate(order[e]):
                nd = self.nodes[i]
                sem, val, inc = info[i]
                deps = list(nd["deps"])
                if use_drains:
                    deps += list(nd["soft"])
                if nd["dma"]:
                    if sem in prev_dma_sem:
                        deps.append(prev_dma_sem[sem])
                    prev_dma_sem[sem] = i
                for d in deps:
                    dsem, dval, dinc = info[d]
                    if dval > waited[e].get(dsem, 0):
                        prog[e].append(("wait", dsem, dval))
                        waited[e][dsem] = dval
                prog[e].append(("call", nd["fn"], sem, inc))
        for item in self.final_raw:
            e, kind = item[0], item[1]
            if kind == "wait":
                s = item[2]
                v = cnt.get(s, item[3])
                prog[e].append(("wait", s, v))
        return prog


def build_nc():
    nc = bass.Bass("TRN2", target_bir_lowering=False)

    xd = nc.dram_tensor("x", [T, D], F32, kind="ExternalInput").ap()
    wq_d = nc.dram_tensor("wq", [D, KD], F32, kind="ExternalInput").ap()
    wk_d = nc.dram_tensor("wk", [D, KD], F32, kind="ExternalInput").ap()
    wv_d = nc.dram_tensor("wv", [D, VD], F32, kind="ExternalInput").ap()
    wg_d = nc.dram_tensor("wg", [D, VD], F32, kind="ExternalInput").ap()
    wgk1_d = nc.dram_tensor("wgk1", [D, 16], F32, kind="ExternalInput").ap()
    wgk2_d = nc.dram_tensor("wgk2", [16, KD], F32, kind="ExternalInput").ap()
    bgk2_d = nc.dram_tensor("bgk2", [128, 4], F32, kind="ExternalInput").ap()
    wout_d = nc.dram_tensor("wout", [VD, D], F32, kind="ExternalInput").ap()
    rmsw_d = nc.dram_tensor("rmsw", [DV], F32, kind="ExternalInput").ap()
    wfcT_d = nc.dram_tensor("wfcT", [D, FF], F32, kind="ExternalInput").ap()
    wprT_d = nc.dram_tensor("wprT", [FF, D], F32, kind="ExternalInput").ap()
    ident_d = nc.dram_tensor("ident", [128, 128], BF16, kind="ExternalInput").ap()
    tri_d = nc.dram_tensor("tri", [128, 128], BF16, kind="ExternalInput").ap()
    outd = nc.dram_tensor("out", [T, D], F32, kind="ExternalOutput").ap()

    from contextlib import ExitStack
    es = ExitStack()
    sb = lambda name, shape, dt: es.enter_context(nc.sbuf_tensor(name, shape, dt)).ap()

    wq = sb("wq_s", [128, 8 * KD], BF16)
    wk = sb("wk_s", [128, 8 * KD], BF16)
    wv = sb("wv_s", [128, 8 * VD], BF16)
    wg = sb("wg_s", [128, 8 * VD], BF16)
    wout = sb("wout_s", [128, 8 * D], BF16)
    wgk1 = sb("wgk1_s", [128, 8 * 16], BF16)
    wgk2 = sb("wgk2_s", [16, KD], BF16)
    wfc2 = [sb("wfc_s0", [128, 8 * 512], BF16), sb("wfc_s1", [128, 8 * 512], BF16)]
    wpr2 = [sb("wpr_s0", [128, 4 * D], BF16), sb("wpr_s1", [128, 4 * D], BF16)]
    ident = sb("ident_s", [128, 128], BF16)
    tri = sb("tri_s", [128, 128], BF16)
    rmsw = sb("rmsw_s", [128, VD], BF16)
    nbgk2 = sb("nbgk2_s", [128, 4], F32)
    eps_col = sb("eps_col", [128, 1], F32)
    one_col = sb("one_col", [128, 1], F32)
    th_col = sb("th_col", [128, 5], F32)
    x2 = [sb("x_sb0", [128, D], F32), sb("x_sb1", [128, D], F32)]
    xn_tm = sb("xn_tm", [128, D], BF16)
    xn_f2 = [sb("xn_fa", [128, 8 * (CPS * C)], BF16),
             sb("xn_fb", [128, 8 * (CPS * C)], BF16)]
    y2 = [sb("y_sba", [128, CPS * D], F32), sb("y_sbb", [128, CPS * D], F32)]
    h_sb = sb("h_sb", [128, 8 * 512], BF16)
    rl2 = [sb("rl0", [128, 512], BF16), sb("rl1", [128, 512], BF16)]
    u_bf = sb("u_bf", [16, 2 * C], BF16)
    ex_sb = sb("ex_sb", [128, KD], BF16)
    sp_sb = sb("sp_sb", [128, KD], F32)
    c_sb = sb("c_sb", [128, KD], F32)
    ct_sb = sb("ct_sb", [128, 4], F32)
    dcy2 = [sb("dcy0", [128, 4], F32), sb("dcy1", [128, 4], F32)]
    er = sb("er", [128, KD], BF16)
    qt2 = [sb("qt0", [128, KD], BF16), sb("qt1", [128, KD], BF16)]
    kt2 = [sb("kt0", [128, KD], BF16), sb("kt1", [128, KD], BF16)]
    kh2 = [sb("kh0", [128, KD], BF16), sb("kh1", [128, KD], BF16)]
    khT = sb("khT", [128, KD], BF16)
    v_bf = sb("v_bf", [128, VD], BF16)
    xwg2 = [sb("xwg0", [128, VD], BF16), sb("xwg1", [128, VD], BF16)]
    o_bf = sb("o_bf", [128, VD], BF16)
    S_bf = sb("S_bf", [128, H * DV], BF16)
    msq = sb("msq", [128, 4], F32)
    rfac = sb("rfac", [128, 4], F32)
    ev = sb("ev", [128, 256], F32)
    sl = sb("sl", [128, 256], F32)
    sg = sb("sg", [128, VD], BF16)
    og = sb("og", [128, VD], BF16)
    ogf = sb("ogf", [128, 8 * 128], BF16)
    sj2 = [sb("sj0", [128, D], BF16)]
    g2 = [sb("g_bf0", [128, D], BF16), sb("g_bf1", [128, D], BF16)]
    ob = sb("ob", [128, 512], F32)
    st_sum = sb("st_sum", [128, 1], F32)
    st_sa = sb("st_sa", [128, 1], F32)
    st_sb_ = sb("st_sb", [128, 1], F32)
    st_mean = sb("st_mean", [128, 1], F32)
    st_var = sb("st_var", [128, 1], F32)
    st_m2 = sb("st_m2", [128, 1], F32)
    st_rstd = sb("st_rstd", [128, 1], F32)
    st_nb = sb("st_nb", [128, 1], F32)

    ps = es.enter_context(nc.psum_tensor("ps", [128, 4096], F32)).ap()
    B = lambda b, o, n: ps[:, b * 512 + o: b * 512 + o + n]

    sems = {}
    semnames = (["s_pe", "s_dve", "s_act", "s_gp", "s_sy"] + Sched.DMA_ROT["sync"]
                + Sched.DMA_ROT["gp"] + Sched.DMA_ROT["out"])
    for s in semnames:
        sems[s] = es.enter_context(nc.semaphore(s))

    S = Sched()
    def dma(dst, src):
        f = lambda e: e.dma_start(dst, src)
        try:
            nb = src.nbytes()
        except Exception:
            nb = 1 << 20
        f.cost = 1500.0 + nb / 400.0
        return f

    def wsrc(d, rows, cols):
        return bass.AP(d.tensor, 0, [[cols, 128], [cols * 128, rows // 128], [1, cols]])
    S.op("gp", dma(wq.rearrange("p (c f) -> p c f", c=8), wsrc(wq_d, D, KD)), writes=[("wq",)])
    S.op("gp", dma(wk.rearrange("p (c f) -> p c f", c=8), wsrc(wk_d, D, KD)), writes=[("wk",)])
    S.op("gp", dma(wv.rearrange("p (c f) -> p c f", c=8), wsrc(wv_d, D, VD)), writes=[("wv",)])
    S.op("gp", dma(wg.rearrange("p (c f) -> p c f", c=8), wsrc(wg_d, D, VD)), writes=[("wg",)])
    S.op("gp", dma(wout.rearrange("p (c f) -> p c f", c=8), wsrc(wout_d, VD, D)), writes=[("wout",)])
    S.op("gp", dma(wgk1.rearrange("p (c f) -> p c f", c=8), wsrc(wgk1_d, D, 16)), writes=[("wgk1",)])
    S.op("gp", dma(wgk2, wgk2_d), writes=[("wgk2",)])
    S.op("gp", dma(rmsw.rearrange("p (h u) -> p h u", h=4),
                   bass.AP(rmsw_d.tensor, 0, [[0, 128], [0, 4], [1, DV]])), writes=[("rmsw",)])
    S.op("sync", dma(ident, ident_d), writes=[("ident",)])
    S.op("sync", dma(tri, tri_d), writes=[("tri",)])
    S.op("sync", dma(nbgk2, bgk2_d), writes=[("nbgk2",)])
    S.op("dve", lambda e: e.tensor_scalar_mul(nbgk2, nbgk2, -1.0), reads=[("nbgk2",)], writes=[("nbgk2",)])
    S.op("dve", lambda e: e.memset(S_bf, 0.0), writes=[("S",)])
    S.op("dve", lambda e: e.memset(eps_col, EPS), writes=[("eps",)])
    S.op("dve", lambda e: e.memset(one_col, 1.0), writes=[("one",)])
    for j in range(5):
        S.op("dve", lambda e, j_=j: e.memset(th_col[:, j_:j_ + 1], -THS[j_]), writes=[("th",)])

    def mmgroup(dst_slices, lhsTs, rhss, flags, warm=0):
        def f(e):
            last = None
            for d_, l_, r_, (st, sp) in zip(dst_slices, lhsTs, rhss, flags):
                last = e.matmul(d_, l_, r_, start=st, stop=sp)
            for _ in range(warm):
                last = e.ldweights(ident)
            return last
        f.cost = sum(max(d_.shape[-1] / 2.4, 100.0) + 15.0 for d_ in dst_slices) + warm * 107.0
        return f

    def chunk_xnf(sc, ci, dc, t0=0, n=C):
        base = dc * (CPS * C) + ci * C + t0
        return xn_f2[sc % 2][:, base: base + n]

    def emit_p1(sc, ci):
        ch = sc * CPS + ci
        t0 = ch * C
        x_sb = x2[ci % 2]
        xk = ("x", ci % 2)
        S.op("sync", dma(x_sb, xd[t0:t0 + C, :]), writes=[xk])
        S.op("dve", lambda e, x_=x_sb: e.tensor_reduce(st_sum, x_, AX.X, ALU.add),
             reads=[xk], writes=[("stsum",)])
        S.op("act", lambda e, x_=x_sb: e.activation(xn_tm[:, 0:512], x_[:, 0:512], AF.Square, accum_out=st_sa),
             reads=[xk], writes=[("xntm",), ("stsa",)])
        S.op("act", lambda e, x_=x_sb: e.activation(xn_tm[:, 512:1024], x_[:, 512:1024], AF.Square, accum_out=st_sb_),
             reads=[xk], writes=[("xntm",), ("stsb",)])
        S.op("dve", lambda e: e.tensor_scalar_mul(st_mean, st_sum, 1.0 / D),
             reads=[("stsum",)], writes=[("stmean",)])
        S.op("dve", lambda e: e.tensor_add(st_var, st_sa, st_sb_),
             reads=[("stsa",), ("stsb",)], writes=[("stvar",)])
        S.op("dve", lambda e: e.tensor_scalar(st_m2, st_mean, st_mean, None, ALU.mult),
             reads=[("stmean",)], writes=[("stm2",)])
        S.op("dve", lambda e: e.tensor_scalar(st_var, st_var, 1.0 / D, None, ALU.mult),
             reads=[("stvar",)], writes=[("stvar",)])
        S.op("dve", lambda e: e.tensor_sub(st_var, st_var, st_m2),
             reads=[("stvar",), ("stm2",)], writes=[("stvar",)])
        S.op("act", lambda e: e.activation(st_rstd, st_var, AF.Ln, bias=eps_col, scale=1.0),
             reads=[("stvar",), ("eps",)], writes=[("strstd",)])
        S.op("act", lambda e: e.activation(st_rstd, st_rstd, AF.Exp, bias=0.0, scale=-0.5),
             reads=[("strstd",)], writes=[("strstd",)])
        S.op("dve", lambda e: e.tensor_scalar(st_nb, st_mean, st_rstd, -1.0, ALU.mult, ALU.mult),
             reads=[("stmean",), ("strstd",)], writes=[("stnb",)])
        S.op("dve", lambda e, x_=x_sb: e.tensor_scalar(xn_tm, x_, st_rstd, st_nb, ALU.mult, ALU.add),
             reads=[xk, ("strstd",), ("stnb",)], writes=[("xntm",)])
        for w in range(2):
            S.op("pe", mmgroup(
                [B(0, dc2 * 128, 128) for dc2 in range(4)],
                [xn_tm[:, (w * 4 + dc2) * 128:(w * 4 + dc2 + 1) * 128] for dc2 in range(4)],
                [ident] * 4, [(True, True)] * 4),
                reads=[("xntm",), ("ident",)], writes=[("ps", 0)])
            if w == 0:
                S.op("act", lambda e, ci_=ci, sc_=sc: [e.activation(chunk_xnf(sc_, ci_, dc2),
                                                            B(0, dc2 * 128, 128), AF.Copy)
                                               for dc2 in range(4)][-1],
                     reads=[("ps", 0)], writes=[("xnfa", sc % 2, ci)])
            else:
                S.op("dve", lambda e, ci_=ci, sc_=sc: [e.tensor_copy(chunk_xnf(sc_, ci_, 4 + dc2),
                                                             B(0, dc2 * 128, 128))
                                               for dc2 in range(4)][-1],
                     reads=[("ps", 0)], writes=[("xnf", sc % 2, ci)])

    def emit_p2pair(u):
        for sli in range(NSL):
            f0 = sli * FSL
            for hh in range(2):
                srch = bass.AP(wfcT_d.tensor, f0 + hh * 512,
                               [[FF, 128], [FF * 128, 8], [1, 512]])
                S.op("gp", dma(wfc2[hh].rearrange("p (c f) -> p c f", c=8), srch),
                     writes=[("wfc", hh)])
                srcp = bass.AP(wprT_d.tensor, (f0 + hh * 512) * D,
                               [[D, 128], [D * 128, 4], [1, D]])
                S.op("gp", dma(wpr2[hh].rearrange("p (c f) -> p c f", c=4), srcp),
                     writes=[("wpr", hh)])
            for sc in (2 * u, 2 * u + 1):
                emit_p2_slice(sc, sli)

    def emit_p2_slice(sc, sli):
            xk = ([("xnf", sc % 2, i) for i in range(4)]
                  + [("xnfa", sc % 2, i) for i in range(4)])
            y_sb = y2[sc % 2]
            for fc in range(8):
                hh = fc // 4
                fcl = fc % 4
                if sli == NSL - 1:
                    mb = 7
                else:
                    mb = 0 if fc % 2 == 0 else 7
                S.op("pe", mmgroup(
                    [B(mb, 0, 512)] * 8,
                    [wfc2[hh][:, dc * 512 + fcl * 128: dc * 512 + (fcl + 1) * 128]
                     for dc in range(8)],
                    [chunk_xnf(sc, 0, dc, 0, 512) for dc in range(8)],
                    [(dc == 0, dc == 7) for dc in range(8)]),
                    reads=[("wfc", hh)] + xk, writes=[("ps", mb)])
                rl = rl2[fc % 2]
                S.op("act", lambda e, rl_=rl, mb_=mb: e.activation(rl_, B(mb_, 0, 512), AF.Relu),
                     reads=[("ps", mb)], writes=[("rl", fc % 2)])
                S.op("gp", lambda e, fc_=fc, rl_=rl: e.tensor_mul(
                    h_sb[:, fc_ * 512:(fc_ + 1) * 512], rl_, rl_),
                    reads=[("rl", fc % 2)], writes=[("h", fc)], gp_compute=True)
            for tt_ in range(4):
                j = tt_
                yb = 1 if tt_ % 2 == 0 else 3
                for fc in range(8):
                    hh = fc // 4
                    fcl = fc % 4
                    dsts, ls, rs, fl = [], [], [], []
                    for hf in range(2):
                        dsts.append(ps[:, yb * 512 + hf * 512: yb * 512 + hf * 512 + 512])
                        ls.append(h_sb[:, fc * 512 + tt_ * 128: fc * 512 + (tt_ + 1) * 128])
                        rs.append(wpr2[hh][:, fcl * D + hf * 512: fcl * D + (hf + 1) * 512])
                        fl.append((fc == 0, fc == 7))
                    S.op("pe", mmgroup(dsts, ls, rs, fl),
                         reads=[("wpr", hh), ("h", fc)],
                         writes=[("ps", yb), ("ps", yb + 1)])
                j_ = j
                if sli == 0:
                    S.op("dve", lambda e, j_=j_, yb_=yb, y_=y_sb: e.tensor_copy(
                        y_[:, j_ * D:(j_ + 1) * D], ps[:, yb_ * 512: yb_ * 512 + 1024]),
                        reads=[("ps", yb), ("ps", yb + 1)],
                        writes=[("y", sc % 2, j, 0), ("y", sc % 2, j, 1)])
                else:
                    S.op("dve", lambda e, j_=j_, yb_=yb, y_=y_sb: e.tensor_tensor(
                        y_[:, j_ * D:(j_ + 1) * D], ps[:, yb_ * 512: yb_ * 512 + 1024],
                        y_[:, j_ * D:(j_ + 1) * D], ALU.add),
                        reads=[("ps", yb), ("ps", yb + 1), ("y", sc % 2, j, 0), ("y", sc % 2, j, 1)],
                        writes=[("y", sc % 2, j, 0), ("y", sc % 2, j, 1)])

    def p3_qk(sc, p):
        ci = 2 * p
        xk = [("xnf", sc % 2, ci), ("xnf", sc % 2, ci + 1),
              ("xnfa", sc % 2, ci), ("xnfa", sc % 2, ci + 1)]
        for name, wmat, bq in (("q", wq, 1), ("k", wk, 3)):
            dsts, ls, rs, fl = [], [], [], []
            for kc in range(4):
                for dc in range(8):
                    dsts.append(ps[:, bq * 512 + kc * 256: bq * 512 + kc * 256 + 256])
                    ls.append(wmat[:, dc * KD + kc * 128: dc * KD + (kc + 1) * 128])
                    rs.append(chunk_xnf(sc, ci, dc, 0, 256))
                    fl.append((dc == 0, dc == 7))
            S.op("pe", mmgroup(dsts, ls, rs, fl),
                 reads=[("wq" if name == "q" else "wk",)] + xk,
                 writes=[("ps", bq), ("ps", bq + 1)])

    def p3_u(sc, p):
        ci = 2 * p
        xk = [("xnf", sc % 2, ci), ("xnf", sc % 2, ci + 1),
              ("xnfa", sc % 2, ci), ("xnfa", sc % 2, ci + 1)]
        S.op("pe", mmgroup(
            [ps[0:16, 0:256]] * 8,
            [wgk1[:, dc * 16:(dc + 1) * 16] for dc in range(8)],
            [chunk_xnf(sc, ci, dc, 0, 256) for dc in range(8)],
            [(dc == 0, dc == 7) for dc in range(8)]),
            reads=[("wgk1",)] + xk, writes=[("ps", 0)])
        S.op("dve", lambda e: e.tensor_copy(u_bf, ps[0:16, 0:256]),
             reads=[("ps", 0)], writes=[("u", 0), ("u", 1)])

    def p3_gates_mm(sc, ci):
        xk = [("xnf", sc % 2, ci), ("xnfa", sc % 2, ci)]
        qt, kt, kh, dcy = qt2[ci % 2], kt2[ci % 2], kh2[ci % 2], dcy2[ci % 2]
        hfu = (ci % 2) * 128
        S.op("pe", mmgroup(
            [B(0, kc * 128, 128) for kc in range(4)],
            [wgk2[:, kc * 128:(kc + 1) * 128] for kc in range(4)],
            [u_bf[:, hfu:hfu + 128]] * 4, [(True, True)] * 4, warm=8),
            reads=[("wgk2",), ("u", ci % 2)], writes=[("ps", 0)])
        S.op("act", lambda e: [e.activation(ex_sb[:, kc * 128:(kc + 1) * 128],
                                            B(0, kc * 128, 128), AF.Exp,
                                            bias=nbgk2[:, kc:kc + 1], scale=-1.0)
                               for kc in range(4)][-1],
             reads=[("ps", 0), ("nbgk2",)], writes=[("exa",)])
        S.op("act", lambda e: e.activation(sp_sb, ex_sb, AF.Ln, bias=one_col, scale=1.0),
             reads=[("exa",), ("one",)], writes=[("sp",)])
        S.op("dve", lambda e: e.tensor_scalar(sp_sb, sp_sb, 1.0 / 16.0, 3.0, ALU.mult, ALU.min),
             reads=[("sp",)], writes=[("sp",)])
        S.op("dve", lambda e: [e.tensor_tensor_scan(c_sb[:, kc * 128:(kc + 1) * 128],
                                                    sp_sb[:, kc * 128:(kc + 1) * 128],
                                                    sp_sb[:, kc * 128:(kc + 1) * 128],
                                                    0.0, ALU.add, ALU.bypass)
                               for kc in range(4)][-1],
             reads=[("sp",)], writes=[("c",)])
        ctv = bass.AP(c_sb.tensor, 127, [[c_sb.ap[0][0], 128], [128, 4]])
        S.op("dve", lambda e, ctv_=ctv: e.tensor_copy(ct_sb, ctv_), reads=[("c",)], writes=[("ct",)])
        S.op("act", lambda e: e.activation(ex_sb, c_sb, AF.Exp, bias=0.0, scale=-1.0),
             reads=[("c",)], writes=[("exa",)])
        S.op("act", lambda e: e.activation(er, c_sb, AF.Exp, bias=0.0, scale=1.0),
             reads=[("c",)], writes=[("er",)])
        S.op("act", lambda e, dcy_=dcy: e.activation(dcy_, ct_sb, AF.Exp, bias=0.0, scale=-1.0),
             reads=[("ct",)], writes=[("dcy", ci % 2)])

    def p3_gates_post(sc, ci):
        qt, kt, kh = qt2[ci % 2], kt2[ci % 2], kh2[ci % 2]
        hf = (ci % 2) * 128
        S.op("dve", lambda e, hf_=hf, qt_=qt: [e.tensor_tensor(
            qt_[:, kc * 128:(kc + 1) * 128],
            ps[:, 512 + kc * 256 + hf_: 512 + kc * 256 + hf_ + 128],
            ex_sb[:, kc * 128:(kc + 1) * 128], ALU.mult) for kc in range(4)][-1],
            reads=[("ps", 1), ("ps", 2), ("exa",)], writes=[("qt", ci % 2)])
        S.op("dve", lambda e, hf_=hf, kt_=kt: [e.tensor_tensor(
            kt_[:, kc * 128:(kc + 1) * 128],
            ps[:, 3 * 512 + kc * 256 + hf_: 3 * 512 + kc * 256 + hf_ + 128],
            er[:, kc * 128:(kc + 1) * 128], ALU.mult) for kc in range(4)][-1],
            reads=[("ps", 3), ("ps", 4), ("er",)], writes=[("kt", ci % 2)])
        dcy_ = dcy2[ci % 2]
        S.op("dve", lambda e, kh_=kh, kt_=kt, d_=dcy_: [e.tensor_scalar(
            kh_[:, kc * 128:(kc + 1) * 128], kt_[:, kc * 128:(kc + 1) * 128],
            d_[:, kc:kc + 1], None, ALU.mult) for kc in range(4)][-1],
            reads=[("kt", ci % 2), ("dcy", ci % 2)], writes=[("kh", ci % 2)])

    def p3_vw(sc, ci):
        xk = [("xnf", sc % 2, ci), ("xnfa", sc % 2, ci)]
        xwg_bf = xwg2[ci % 2]
        for name, wmat, buf, bkey in (("v", wv, v_bf, ("v",)), ("w", wg, xwg_bf, ("xwg", ci % 2))):
            dsts, ls, rs, fl = [], [], [], []
            for hf in range(2):
                for dc in range(8):
                    dsts.append(B(5, 0, 1024)[:, hf * 512:(hf + 1) * 512])
                    ls.append(chunk_xnf(sc, ci, dc))
                    rs.append(wmat[:, dc * VD + hf * 512: dc * VD + (hf + 1) * 512])
                    fl.append((dc == 0, dc == 7))
            S.op("pe", mmgroup(dsts, ls, rs, fl),
                 reads=[("wv" if name == "v" else "wg",)] + xk, writes=[("ps", 5), ("ps", 6)])
            if name == "v":
                S.op("dve", lambda e, buf_=buf: e.tensor_copy(buf_, B(5, 0, 1024)),
                     reads=[("ps", 5), ("ps", 6)], writes=[bkey])
            else:
                S.op("act", lambda e, buf_=buf: e.activation(buf_, B(5, 0, 1024), AF.Copy),
                     reads=[("ps", 5), ("ps", 6)], writes=[bkey])

    def p3_mid(sc, ci):
        qt, kt, kh, dcy = qt2[ci % 2], kt2[ci % 2], kh2[ci % 2], dcy2[ci % 2]
        pt = kt
        S.op("pe", mmgroup(
            [B(7, h * 128, 128) for h in range(H)],
            [kh[:, h * 128:(h + 1) * 128] for h in range(H)],
            [ident] * H, [(True, True)] * H, warm=8),
            reads=[("kh", ci % 2), ("ident",)], writes=[("ps", 7)])
        S.op("dve", lambda e: e.tensor_copy(khT, B(7, 0, 512)),
             reads=[("ps", 7)], writes=[("khT",)])
        S.op("pe", mmgroup(
            [B(7, h * 128, 128) for h in range(H)],
            [kt[:, h * 128:(h + 1) * 128] for h in range(H)],
            [qt[:, h * 128:(h + 1) * 128] for h in range(H)],
            [(True, True)] * H, warm=8),
            reads=[("kt", ci % 2), ("qt", ci % 2)], writes=[("ps", 7)])
        S.op("dve", lambda e: [e.tensor_tensor(pt[:, h * 128:(h + 1) * 128],
                                               B(7, h * 128, 128), tri, ALU.mult)
                               for h in range(H)][-1],
             reads=[("ps", 7), ("tri",)], writes=[("kt", ci % 2)])
        dsts, ls, rs, fl = [], [], [], []
        for h in range(H):
            dsts += [B(5, 0, 1024)[:, h * 256:(h + 1) * 256]] * 2
            ls += [pt[:, h * 128:(h + 1) * 128], qt[:, h * 128:(h + 1) * 128]]
            rs += [v_bf[:, h * 256:(h + 1) * 256], S_bf[:, h * 256:(h + 1) * 256]]
            fl += [(True, False), (False, True)]
        S.op("pe", mmgroup(dsts, ls, rs, fl, warm=8),
             reads=[("kt", ci % 2), ("qt", ci % 2), ("v",), ("S",)], writes=[("ps", 5), ("ps", 6)])
        S.op("dve", lambda e: e.tensor_copy(o_bf, B(5, 0, 1024)),
             reads=[("ps", 5), ("ps", 6)], writes=[("o",)])
        S.op("pe", mmgroup(
            [B(5, 0, 1024)[:, h * 256:(h + 1) * 256] for h in range(H)],
            [khT[:, h * 128:(h + 1) * 128] for h in range(H)],
            [v_bf[:, h * 256:(h + 1) * 256] for h in range(H)],
            [(True, True)] * H, warm=8),
            reads=[("khT",), ("v",)], writes=[("ps", 5), ("ps", 6)])
        S.op("dve", lambda e, dcy_=dcy: [e.scalar_tensor_tensor(S_bf[:, h * 256:(h + 1) * 256],
                                                      S_bf[:, h * 256:(h + 1) * 256],
                                                      dcy_[:, h:h + 1],
                                                      B(5, 0, 1024)[:, h * 256:(h + 1) * 256],
                                                      ALU.mult, ALU.add)
                               for h in range(H)][-1],
             reads=[("ps", 5), ("ps", 6), ("dcy", ci % 2), ("S",)], writes=[("S",)])

    def p3_silu(sc, ci):
        xwg_bf = xwg2[ci % 2]
        xwk = ("xwg", ci % 2)
        for hh in range(4):
            S.op("act", lambda e, hh_=hh, x_=xwg_bf: e.activation(
                ev, x_[:, hh_ * 256:(hh_ + 1) * 256], AF.Exp),
                reads=[xwk], writes=[("ev",)])
            S.op("act", lambda e: e.activation(sl, ev, AF.Ln, bias=one_col, scale=1.0),
                 reads=[("ev",), ("one",)], writes=[("sl",)])
            S.op("dve", lambda e, hh_=hh, x_=xwg_bf: e.tensor_sub(
                ev, x_[:, hh_ * 256:(hh_ + 1) * 256], sl),
                reads=[xwk, ("sl",)], writes=[("ev",)])
            S.op("act", lambda e, hh_=hh: e.activation(sg[:, hh_ * 256:(hh_ + 1) * 256],
                                                       ev, AF.Exp),
                 reads=[("ev",)], writes=[("sg", hh % 2)])

    def p3_epi_pre(sc, ci):
        xwg_bf = xwg2[ci % 2]
        xwk = ("xwg", ci % 2)
        S.op("act", lambda e: [e.activation(og[:, h * 256:(h + 1) * 256],
                                            o_bf[:, h * 256:(h + 1) * 256],
                                            AF.Square, accum_out=msq[:, h:h + 1])
                               for h in range(H)][-1],
             reads=[("o",)], writes=[("msq",), ("og",)])
        S.op("act", lambda e: e.activation(rfac, msq, AF.Ln, bias=eps_col, scale=1.0 / DV),
             reads=[("msq",), ("eps",)], writes=[("rfac",)])
        S.op("act", lambda e: e.activation(rfac, rfac, AF.Exp, bias=0.0, scale=-0.5),
             reads=[("rfac",)], writes=[("rfac",)])
        S.op("dve", lambda e: e.tensor_tensor(sg, sg, rmsw, ALU.mult),
             reads=[("sg", 0), ("sg", 1), ("rmsw",)], writes=[("sg", 0), ("sg", 1)])
        S.op("dve", lambda e, x_=xwg_bf: e.tensor_tensor(sg, sg, x_, ALU.mult),
             reads=[("sg", 0), ("sg", 1), xwk], writes=[("sg", 0), ("sg", 1)])
        S.op("dve", lambda e: [e.scalar_tensor_tensor(
            og[:, h * 256:(h + 1) * 256],
            o_bf[:, h * 256:(h + 1) * 256], rfac[:, h:h + 1],
            sg[:, h * 256:(h + 1) * 256], ALU.mult, ALU.mult)
            for h in range(H)][-1],
            reads=[("o",), ("rfac",), ("sg", 0), ("sg", 1)], writes=[("og",)])

    def p3_epi_ogt(sc, ci):
        for w in range(2):
            S.op("pe", mmgroup(
                [B(7, vc2 * 128, 128) for vc2 in range(4)],
                [og[:, (w * 4 + vc2) * 128:(w * 4 + vc2 + 1) * 128] for vc2 in range(4)],
                [ident] * 4, [(True, True)] * 4, warm=8),
                reads=[("og",), ("ident",)], writes=[("ps", 7)])
            if w == 0:
                S.op("act", lambda e: e.activation(ogf[:, 0:512], B(7, 0, 512), AF.Copy),
                     reads=[("ps", 7)], writes=[("ogfa",)])
            else:
                S.op("dve", lambda e: e.tensor_copy(ogf[:, 512:1024], B(7, 0, 512)),
                     reads=[("ps", 7)], writes=[("ogf",)])

    def p3_spike(sc, ci):
        yk = [("y", sc % 2, ci, 0), ("y", sc % 2, ci, 1)]
        yi = y2[sc % 2][:, ci * D:(ci + 1) * D]
        gb = g2[ci % 2]
        for j in range(5):
            sjb = sj2[0]
            S.op("act", lambda e, j_=j, s_=sjb, yi_=yi: e.activation(
                s_, yi_, AF.Sign, bias=th_col[:, j_:j_ + 1], scale=1.0),
                reads=yk + [("th",)], writes=[("sj", 0)])
            if j == 0:
                S.op("dve", lambda e, s_=sjb, gb_=gb: e.tensor_scalar(gb_, s_, DS2[0], C0, ALU.mult, ALU.add),
                     reads=[("sj", 0)], writes=[("g", ci % 2)])
            else:
                S.op("dve", lambda e, j_=j, s_=sjb, gb_=gb: e.scalar_tensor_tensor(
                    gb_, s_, DS2[j_], gb_, ALU.mult, ALU.add),
                    reads=[("sj", 0), ("g", ci % 2)], writes=[("g", ci % 2)])

    def p3_tail(sc, ci):
        ch = sc * CPS + ci
        t0 = ch * C
        bq = 1 if ci % 2 == 0 else 3
        for half in range(2):
            dsts, ls, rs, fl = [], [], [], []
            for hf in range(2):
                for vc in range(half * 4, half * 4 + 4):
                    dsts.append(ps[:, bq * 512 + hf * 512: bq * 512 + hf * 512 + 512])
                    ls.append(ogf[:, vc * 128:(vc + 1) * 128])
                    rs.append(wout[:, vc * D + hf * 512: vc * D + (hf + 1) * 512])
                    fl.append((vc == half * 4 and half == 0, vc == half * 4 + 3 and half == 1))
            S.op("pe", mmgroup(dsts, ls, rs, fl),
                 reads=[("ogfa",) if half == 0 else ("ogf",), ("wout",)],
                 writes=[("ps", bq), ("ps", bq + 1)])
        yslot = y2[sc % 2][:, ci * D:(ci + 1) * D]
        S.op("sync", dma(yslot, xd[t0:t0 + C, :]),
             writes=[("y", sc % 2, ci, 0), ("y", sc % 2, ci, 1)])
        gb = g2[ci % 2]
        for hh in range(2):
            yk1 = ("y", sc % 2, ci, hh)
            S.op("dve", lambda e, bq_=bq, gb_=gb, hh_=hh: e.scalar_tensor_tensor(
                ob, ps[:, bq_ * 512 + hh_ * 512: bq_ * 512 + hh_ * 512 + 512], 0.0,
                gb_[:, hh_ * 512:(hh_ + 1) * 512], ALU.bypass, ALU.mult),
                reads=[("ps", bq + hh), ("g", ci % 2)], writes=[("ob",)])
            S.op("dve", lambda e, y_=yslot, hh_=hh: e.tensor_tensor(
                y_[:, hh_ * 512:(hh_ + 1) * 512], ob,
                y_[:, hh_ * 512:(hh_ + 1) * 512], ALU.add),
                reads=[("ob",), yk1], writes=[yk1])
            S.op("sync", lambda e, t0_=t0, hh_=hh, y_=yslot: e.dma_start(
                outd[t0_:t0_ + C, hh_ * 512:(hh_ + 1) * 512], y_[:, hh_ * 512:(hh_ + 1) * 512]),
                reads=[yk1], writes=[("outdma", ch, hh)], out_dma=True)

    def emit_p3(sc):
        p3_u(sc, 0)
        p3_gates_mm(sc, 0)
        p3_vw(sc, 0)
        for p in range(CPS // 2):
            A, Bc = 2 * p, 2 * p + 1
            p3_qk(sc, p)
            p3_gates_post(sc, A)
            p3_gates_mm(sc, Bc)
            p3_mid(sc, A)
            p3_gates_post(sc, Bc)
            p3_silu(sc, A)
            p3_epi_pre(sc, A)
            p3_vw(sc, Bc)
            p3_spike(sc, A)
            p3_mid(sc, Bc)
            p3_silu(sc, Bc)
            p3_epi_ogt(sc, A)
            p3_epi_pre(sc, Bc)
            p3_tail(sc, A)
            if p < CPS // 2 - 1:
                p3_u(sc, p + 1)
                p3_gates_mm(sc, 2 * p + 2)
                p3_vw(sc, 2 * p + 2)
            p3_spike(sc, Bc)
            p3_epi_ogt(sc, Bc)
            p3_tail(sc, Bc)

    for u in range(NSUP // 2):
        for sc in (2 * u, 2 * u + 1):
            for ci in range(CPS):
                emit_p1(sc, ci)
        emit_p2pair(u)
        emit_p3(2 * u)
        emit_p3(2 * u + 1)

    for s in Sched.DMA_ROT["out"]:
        S.raw("sync", "wait", s, 0)

    import os as _os
    prog = S.emit(nc, sems, reorder=_os.environ.get("K_NO_REORDER") != "1")
    with nc.Block() as block:
        def runner(eng_name):
            def run(e):
                for item in prog[eng_name]:
                    if item[0] == "wait":
                        e.wait_ge(sems[item[1]], item[2])
                    elif item[0] == "drain":
                        e.drain()
                    else:
                        _, fn, sem, inc = item
                        inst = fn(e)
                        if isinstance(inst, list):
                            inst = inst[-1]
                        if inc:
                            inst.then_inc(sems[sem], inc)
            return run
        block.tensor(runner("pe"))
        block.vector(runner("dve"))
        block.scalar(runner("act"))
        block.sync(runner("sync"))
        block.gpsimd(runner("gp"))

    es.close()
    return nc


_NC_CACHE = None


def kernel(L0_attn_x, L0_attn_Wq, L0_attn_Wk, L0_attn_Wv, L0_attn_Wg,
           L0_attn_Wgk1, L0_attn_Wgk2, L0_attn_bgk2, L0_attn_Wout,
           L0_attn_rms_w, L0_ffn_W_fc, L0_ffn_W_proj):
    global _NC_CACHE
    n = 8
    x = np.ascontiguousarray(np.asarray(L0_attn_x, dtype=np.float32))
    f32 = lambda a: np.ascontiguousarray(np.asarray(a, dtype=np.float32))
    com = {
        "wq": f32(L0_attn_Wq), "wk": f32(L0_attn_Wk), "wv": f32(L0_attn_Wv),
        "wg": f32(L0_attn_Wg), "wgk1": f32(L0_attn_Wgk1), "wgk2": f32(L0_attn_Wgk2),
        "bgk2": np.ascontiguousarray(f32(L0_attn_bgk2).reshape(4, 128).T),
        "wout": f32(L0_attn_Wout), "rmsw": f32(L0_attn_rms_w),
        "wfcT": np.ascontiguousarray(f32(L0_ffn_W_fc).T),
        "wprT": np.ascontiguousarray(f32(L0_ffn_W_proj).T),
        "ident": np.eye(128, dtype=ml_dtypes.bfloat16),
        "tri": np.triu(np.ones((128, 128))).astype(ml_dtypes.bfloat16),
    }
    if _NC_CACHE is None:
        _NC_CACHE = build_nc()
    nc = _NC_CACHE
    in_maps = [dict(com, x=np.ascontiguousarray(x[b])) for b in range(n)]
    res = run_bass_kernel_spmd(nc, in_maps, core_ids=list(range(n)))
    return np.stack([np.asarray(r["out"], dtype=np.float32) for r in res.results], axis=0)


# version 39
# speedup vs baseline: 1.0683x; 1.0230x over previous
import numpy as np
import ml_dtypes
import concourse.bass as bass
import concourse.mybir as mybir
from concourse.bass_utils import run_bass_kernel_spmd

F32 = mybir.dt.float32
BF16 = mybir.dt.bfloat16
AF = mybir.ActivationFunctionType
ALU = mybir.AluOpType
AX = mybir.AxisListType

T, D, H, DK, DV, KD, VD, FF = 2048, 1024, 4, 128, 256, 512, 1024, 4096
C = 128
NCH = T // C
NSUP = 4
CPS = NCH // NSUP
NSL = 4
FSL = FF // NSL
EPS = 1e-5

THS = [0.24419428096993973, 0.4639691338428854, 0.6861859295255306,
       0.9059607823984763, 1.0839784132255623]
DS2 = [0.1099667994624956 / 2, 0.1295082645824852 / 2, 0.14791871969049653 / 2,
       0.16458580116661264 / 2, 0.17907899372791491 / 2]
C0 = sum(DS2)

ENGS = ("pe", "dve", "act", "sync", "gp")


class Sched:

    DMA_ROT = {"sync": ["s_hd%d" % i for i in range(8)],
               "gp": ["s_wd%d" % i for i in range(4)],
               "out": ["s_ot%d" % i for i in range(4)]}
    SEM_LAT = 300.0

    def __init__(self):
        self.nodes = []
        self.wmap = {}
        self.rmap = {}
        self.ps_reader = {}
        self.final_raw = []

    def op(self, eng, fn, reads=(), writes=(), out_dma=False, gp_compute=False,
           cost=850.0):
        deps = set()
        soft = set()
        for k in reads:
            if k in self.wmap:
                deps.add(self.wmap[k])
            if isinstance(k, tuple) and k and k[0] == "ps":
                lr = self.ps_reader.get(k)
                if lr is not None and lr[1] != eng:
                    deps.add(lr[0])
        for k in writes:
            if k in self.wmap:
                d = self.wmap[k]
                (soft if self.nodes[d]["eng"] == eng and not self.nodes[d]["dma"]
                 else deps).add(d)
            for d in self.rmap.get(k, ()):
                (soft if self.nodes[d]["eng"] == eng and not self.nodes[d]["dma"]
                 else deps).add(d)
        soft -= deps
        nid = len(self.nodes)
        cost = float(getattr(fn, "cost", cost))
        self.nodes.append(dict(eng=eng, fn=fn, deps=deps, soft=soft, cost=cost,
                               out_dma=out_dma,
                               dma=(eng in ("sync", "gp") and not gp_compute)))
        for k in writes:
            self.wmap[k] = nid
            self.rmap[k] = []
        for k in reads:
            self.rmap.setdefault(k, []).append(nid)
            if isinstance(k, tuple) and k and k[0] == "ps":
                self.ps_reader[k] = (nid, eng)
        for k in writes:
            self.ps_reader.pop(k, None)
        return nid

    def raw(self, eng, kind, *args):
        self.final_raw.append((eng, kind) + args)

    def schedule(self, reorder=True):
        n = len(self.nodes)
        order = {e: [] for e in ENGS}
        if not reorder:
            for i, nd in enumerate(self.nodes):
                order[nd["eng"]].append(i)
            return order
        import heapq
        ndep = [len(nd["deps"]) + len(nd["soft"]) for nd in self.nodes]
        kids = [[] for _ in range(n)]
        for i, nd in enumerate(self.nodes):
            for d in nd["deps"] | nd["soft"]:
                kids[d].append(i)
        ready = {e: [] for e in ENGS}
        rtime = [0.0] * n
        for i, nd in enumerate(self.nodes):
            if ndep[i] == 0:
                heapq.heappush(ready[nd["eng"]], i)
        eng_t = {e: 0.0 for e in ENGS}
        done_t = [0.0] * n
        scheduled = 0
        evq = []
        while scheduled < n:
            progressed = False
            for e in ENGS:
                if ready[e]:
                    i = heapq.heappop(ready[e])
                    nd = self.nodes[i]
                    st = max(eng_t[e], rtime[i])
                    ft = st + nd["cost"]
                    eng_t[e] = st + (200.0 if nd["dma"] else nd["cost"])
                    done_t[i] = ft
                    order[e].append(i)
                    heapq.heappush(evq, (ft, i))
                    scheduled += 1
                    progressed = True
            if not progressed:
                if not evq:
                    raise RuntimeError("scheduler stuck")
                ft, i = heapq.heappop(evq)
                for k in kids[i]:
                    ndep[k] -= 1
                    rtime[k] = max(rtime[k], ft + self.SEM_LAT)
                    if ndep[k] == 0:
                        heapq.heappush(ready[self.nodes[k]["eng"]], k)
                for e in ENGS:
                    eng_t[e] = max(eng_t[e], min(eng_t[e] + 0, ft))
        return order

    def emit(self, nc, sems, reorder=True):
        import os as _os
        use_drains = (_os.environ.get("BASS_SIM_FP64") == "1"
                      or _os.environ.get("K_SIM") == "1")
        order = self.schedule(reorder)
        pos = {}
        for e in ENGS:
            for p, i in enumerate(order[e]):
                pos[i] = (e, p)
        needs_inc = set()
        for i, nd in enumerate(self.nodes):
            if nd["dma"]:
                needs_inc.add(i)
            for d in nd["deps"]:
                needs_inc.add(d)
            if use_drains:
                for d in nd["soft"]:
                    needs_inc.add(d)
        info = {}
        cnt = {}
        rot_i = {"sync": 0, "gp": 0, "out": 0}
        for e in ENGS:
            for i in order[e]:
                nd = self.nodes[i]
                if nd["dma"]:
                    rk = "out" if nd["out_dma"] else e
                    sem = self.DMA_ROT[rk][rot_i[rk] % len(self.DMA_ROT[rk])]
                    rot_i[rk] += 1
                    inc = 16
                else:
                    sem = {"pe": "s_pe", "dve": "s_dve", "act": "s_act",
                           "gp": "s_gp", "sync": "s_sy"}[e]
                    inc = 1 if i in needs_inc else 0
                val = cnt.get(sem, 0) + inc
                cnt[sem] = val
                info[i] = (sem, val, inc)
        prog = {e: [] for e in ENGS}
        waited = {e: {} for e in ENGS}
        drain_upto = {e: -1 for e in ENGS}
        prev_dma_sem = {}
        for e in ENGS:
            for p, i in enumerate(order[e]):
                nd = self.nodes[i]
                sem, val, inc = info[i]
                deps = list(nd["deps"])
                if use_drains:
                    deps += list(nd["soft"])
                if nd["dma"]:
                    if sem in prev_dma_sem:
                        deps.append(prev_dma_sem[sem])
                    prev_dma_sem[sem] = i
                for d in deps:
                    dsem, dval, dinc = info[d]
                    if dval > waited[e].get(dsem, 0):
                        prog[e].append(("wait", dsem, dval))
                        waited[e][dsem] = dval
                prog[e].append(("call", nd["fn"], sem, inc))
        for item in self.final_raw:
            e, kind = item[0], item[1]
            if kind == "wait":
                s = item[2]
                v = cnt.get(s, item[3])
                prog[e].append(("wait", s, v))
        return prog


def build_nc():
    nc = bass.Bass("TRN2", target_bir_lowering=False)

    xd = nc.dram_tensor("x", [T, D], F32, kind="ExternalInput").ap()
    wq_d = nc.dram_tensor("wq", [D, KD], F32, kind="ExternalInput").ap()
    wk_d = nc.dram_tensor("wk", [D, KD], F32, kind="ExternalInput").ap()
    wv_d = nc.dram_tensor("wv", [D, VD], F32, kind="ExternalInput").ap()
    wg_d = nc.dram_tensor("wg", [D, VD], F32, kind="ExternalInput").ap()
    wgk1_d = nc.dram_tensor("wgk1", [D, 16], F32, kind="ExternalInput").ap()
    wgk2_d = nc.dram_tensor("wgk2", [16, KD], F32, kind="ExternalInput").ap()
    bgk2_d = nc.dram_tensor("bgk2", [128, 4], F32, kind="ExternalInput").ap()
    wout_d = nc.dram_tensor("wout", [VD, D], F32, kind="ExternalInput").ap()
    rmsw_d = nc.dram_tensor("rmsw", [DV], F32, kind="ExternalInput").ap()
    wfcT_d = nc.dram_tensor("wfcT", [D, FF], F32, kind="ExternalInput").ap()
    wprT_d = nc.dram_tensor("wprT", [FF, D], F32, kind="ExternalInput").ap()
    ident_d = nc.dram_tensor("ident", [128, 128], BF16, kind="ExternalInput").ap()
    tri_d = nc.dram_tensor("tri", [128, 128], BF16, kind="ExternalInput").ap()
    outd = nc.dram_tensor("out", [T, D], F32, kind="ExternalOutput").ap()

    from contextlib import ExitStack
    es = ExitStack()
    sb = lambda name, shape, dt: es.enter_context(nc.sbuf_tensor(name, shape, dt)).ap()

    wq = sb("wq_s", [128, 8 * KD], BF16)
    wk = sb("wk_s", [128, 8 * KD], BF16)
    wv = sb("wv_s", [128, 8 * VD], BF16)
    wg = sb("wg_s", [128, 8 * VD], BF16)
    wout = sb("wout_s", [128, 8 * D], BF16)
    wgk1 = sb("wgk1_s", [128, 8 * 16], BF16)
    wgk2 = sb("wgk2_s", [16, KD], BF16)
    wfc2 = [sb("wfc_s0", [128, 8 * 512], BF16), sb("wfc_s1", [128, 8 * 512], BF16)]
    wpr2 = [sb("wpr_s0", [128, 4 * D], BF16), sb("wpr_s1", [128, 4 * D], BF16)]
    ident = sb("ident_s", [128, 128], BF16)
    tri = sb("tri_s", [128, 128], BF16)
    rmsw = sb("rmsw_s", [128, VD], BF16)
    nbgk2 = sb("nbgk2_s", [128, 4], F32)
    eps_col = sb("eps_col", [128, 1], F32)
    one_col = sb("one_col", [128, 1], F32)
    th_col = sb("th_col", [128, 5], F32)
    x2 = [sb("x_sb0", [128, D], F32), sb("x_sb1", [128, D], F32)]
    xn_tm = sb("xn_tm", [128, D], BF16)
    xn_f2 = [sb("xn_fa", [128, 8 * (CPS * C)], BF16),
             sb("xn_fb", [128, 8 * (CPS * C)], BF16)]
    y2 = [sb("y_sba", [128, CPS * D], F32), sb("y_sbb", [128, CPS * D], F32)]
    h_sb = sb("h_sb", [128, 8 * 512], BF16)
    rl2 = [sb("rl0", [128, 512], BF16), sb("rl1", [128, 512], BF16)]
    u_bf = sb("u_bf", [16, 2 * C], BF16)
    ex_sb = sb("ex_sb", [128, KD], BF16)
    sp_sb = sb("sp_sb", [128, KD], F32)
    c_sb = sb("c_sb", [128, KD], F32)
    ct2g = [sb("ct_sb", [128, 4], F32), sb("ct_sb1", [128, 4], F32)]
    ct_sb = ct2g[0]
    dcy2 = [sb("dcy0", [128, 4], F32), sb("dcy1", [128, 4], F32)]
    er = sb("er", [128, KD], BF16)
    qt2 = [sb("qt0", [128, KD], BF16), sb("qt1", [128, KD], BF16)]
    kt2 = [sb("kt0", [128, KD], BF16), sb("kt1", [128, KD], BF16)]
    kh2 = [sb("kh0", [128, KD], BF16), sb("kh1", [128, KD], BF16)]
    khT = sb("khT", [128, KD], BF16)
    v_bf = sb("v_bf", [128, VD], BF16)
    xwg2 = [sb("xwg0", [128, VD], BF16), sb("xwg1", [128, VD], BF16)]
    o_bf = sb("o_bf", [128, VD], BF16)
    S_bf = sb("S_bf", [128, H * DV], BF16)
    msq = sb("msq", [128, 4], F32)
    rfac = sb("rfac", [128, 4], F32)
    ev = sb("ev", [128, 256], F32)
    sl = sb("sl", [128, 256], F32)
    sg = sb("sg", [128, VD], BF16)
    og = sb("og", [128, VD], BF16)
    ogf = sb("ogf", [128, 8 * 128], BF16)
    sj2 = [sb("sj0", [128, D], BF16)]
    g2 = [sb("g_bf0", [128, D], BF16), sb("g_bf1", [128, D], BF16)]
    ob = sb("ob", [128, 512], F32)
    st_sum = sb("st_sum", [128, 1], F32)
    st_sa = sb("st_sa", [128, 1], F32)
    st_sb_ = sb("st_sb", [128, 1], F32)
    st_mean = sb("st_mean", [128, 1], F32)
    st_var = sb("st_var", [128, 1], F32)
    st_m2 = sb("st_m2", [128, 1], F32)
    st_rstd = sb("st_rstd", [128, 1], F32)
    st_nb = sb("st_nb", [128, 1], F32)

    hbc = h_sb.bitcast(F32)
    ex2g = [(ex_sb, []), (h_sb[:, 0:512], [("h", 0)])]
    er2g = [(er, []), (h_sb[:, 512:1024], [("h", 1)])]
    sp2g = [(sp_sb, []), (hbc[:, 512:1024], [("h", 2), ("h", 3)])]
    c2g = [(c_sb, []), (hbc[:, 1024:1536], [("h", 4), ("h", 5)])]
    ps = es.enter_context(nc.psum_tensor("ps", [128, 4096], F32)).ap()
    B = lambda b, o, n: ps[:, b * 512 + o: b * 512 + o + n]

    sems = {}
    semnames = (["s_pe", "s_dve", "s_act", "s_gp", "s_sy"] + Sched.DMA_ROT["sync"]
                + Sched.DMA_ROT["gp"] + Sched.DMA_ROT["out"])
    for s in semnames:
        sems[s] = es.enter_context(nc.semaphore(s))

    S = Sched()
    def dma(dst, src):
        f = lambda e: e.dma_start(dst, src)
        try:
            nb = src.nbytes()
        except Exception:
            nb = 1 << 20
        f.cost = 1500.0 + nb / 400.0
        return f

    def wsrc(d, rows, cols):
        return bass.AP(d.tensor, 0, [[cols, 128], [cols * 128, rows // 128], [1, cols]])
    S.op("gp", dma(wq.rearrange("p (c f) -> p c f", c=8), wsrc(wq_d, D, KD)), writes=[("wq",)])
    S.op("gp", dma(wk.rearrange("p (c f) -> p c f", c=8), wsrc(wk_d, D, KD)), writes=[("wk",)])
    S.op("gp", dma(wv.rearrange("p (c f) -> p c f", c=8), wsrc(wv_d, D, VD)), writes=[("wv",)])
    S.op("gp", dma(wg.rearrange("p (c f) -> p c f", c=8), wsrc(wg_d, D, VD)), writes=[("wg",)])
    S.op("gp", dma(wout.rearrange("p (c f) -> p c f", c=8), wsrc(wout_d, VD, D)), writes=[("wout",)])
    S.op("gp", dma(wgk1.rearrange("p (c f) -> p c f", c=8), wsrc(wgk1_d, D, 16)), writes=[("wgk1",)])
    S.op("gp", dma(wgk2, wgk2_d), writes=[("wgk2",)])
    S.op("gp", dma(rmsw.rearrange("p (h u) -> p h u", h=4),
                   bass.AP(rmsw_d.tensor, 0, [[0, 128], [0, 4], [1, DV]])), writes=[("rmsw",)])
    S.op("sync", dma(ident, ident_d), writes=[("ident",)])
    S.op("sync", dma(tri, tri_d), writes=[("tri",)])
    S.op("sync", dma(nbgk2, bgk2_d), writes=[("nbgk2",)])
    S.op("dve", lambda e: e.tensor_scalar_mul(nbgk2, nbgk2, -1.0), reads=[("nbgk2",)], writes=[("nbgk2",)])
    S.op("dve", lambda e: e.memset(S_bf, 0.0), writes=[("S",)])
    S.op("dve", lambda e: e.memset(eps_col, EPS), writes=[("eps",)])
    S.op("dve", lambda e: e.memset(one_col, 1.0), writes=[("one",)])
    for j in range(5):
        S.op("dve", lambda e, j_=j: e.memset(th_col[:, j_:j_ + 1], -THS[j_]), writes=[("th",)])

    def mmgroup(dst_slices, lhsTs, rhss, flags, warm=0):
        def f(e):
            last = None
            for d_, l_, r_, (st, sp) in zip(dst_slices, lhsTs, rhss, flags):
                last = e.matmul(d_, l_, r_, start=st, stop=sp)
            for _ in range(warm):
                last = e.ldweights(ident)
            return last
        f.cost = sum(max(d_.shape[-1] / 2.4, 100.0) + 15.0 for d_ in dst_slices) + warm * 107.0
        return f

    def chunk_xnf(sc, ci, dc, t0=0, n=C):
        base = dc * (CPS * C) + ci * C + t0
        return xn_f2[sc % 2][:, base: base + n]

    def emit_p1(sc, ci):
        ch = sc * CPS + ci
        t0 = ch * C
        x_sb = x2[ci % 2]
        xk = ("x", ci % 2)
        S.op("sync", dma(x_sb, xd[t0:t0 + C, :]), writes=[xk])
        S.op("dve", lambda e, x_=x_sb: e.tensor_reduce(st_sum, x_, AX.X, ALU.add),
             reads=[xk], writes=[("stsum",)])
        S.op("act", lambda e, x_=x_sb: e.activation(xn_tm[:, 0:512], x_[:, 0:512], AF.Square, accum_out=st_sa),
             reads=[xk], writes=[("xntm",), ("stsa",)])
        S.op("act", lambda e, x_=x_sb: e.activation(xn_tm[:, 512:1024], x_[:, 512:1024], AF.Square, accum_out=st_sb_),
             reads=[xk], writes=[("xntm",), ("stsb",)])
        S.op("dve", lambda e: e.tensor_scalar_mul(st_mean, st_sum, 1.0 / D),
             reads=[("stsum",)], writes=[("stmean",)])
        S.op("dve", lambda e: e.tensor_add(st_var, st_sa, st_sb_),
             reads=[("stsa",), ("stsb",)], writes=[("stvar",)])
        S.op("dve", lambda e: e.tensor_scalar(st_m2, st_mean, st_mean, None, ALU.mult),
             reads=[("stmean",)], writes=[("stm2",)])
        S.op("dve", lambda e: e.tensor_scalar(st_var, st_var, 1.0 / D, None, ALU.mult),
             reads=[("stvar",)], writes=[("stvar",)])
        S.op("dve", lambda e: e.tensor_sub(st_var, st_var, st_m2),
             reads=[("stvar",), ("stm2",)], writes=[("stvar",)])
        S.op("act", lambda e: e.activation(st_rstd, st_var, AF.Ln, bias=eps_col, scale=1.0),
             reads=[("stvar",), ("eps",)], writes=[("strstd",)])
        S.op("act", lambda e: e.activation(st_rstd, st_rstd, AF.Exp, bias=0.0, scale=-0.5),
             reads=[("strstd",)], writes=[("strstd",)])
        S.op("dve", lambda e: e.tensor_scalar(st_nb, st_mean, st_rstd, -1.0, ALU.mult, ALU.mult),
             reads=[("stmean",), ("strstd",)], writes=[("stnb",)])
        S.op("dve", lambda e, x_=x_sb: e.tensor_scalar(xn_tm, x_, st_rstd, st_nb, ALU.mult, ALU.add),
             reads=[xk, ("strstd",), ("stnb",)], writes=[("xntm",)])
        for w in range(2):
            S.op("pe", mmgroup(
                [B(0, dc2 * 128, 128) for dc2 in range(4)],
                [xn_tm[:, (w * 4 + dc2) * 128:(w * 4 + dc2 + 1) * 128] for dc2 in range(4)],
                [ident] * 4, [(True, True)] * 4),
                reads=[("xntm",), ("ident",)], writes=[("ps", 0)])
            if w == 0:
                S.op("act", lambda e, ci_=ci, sc_=sc: [e.activation(chunk_xnf(sc_, ci_, dc2),
                                                            B(0, dc2 * 128, 128), AF.Copy)
                                               for dc2 in range(4)][-1],
                     reads=[("ps", 0)], writes=[("xnfa", sc % 2, ci)])
            else:
                S.op("dve", lambda e, ci_=ci, sc_=sc: [e.tensor_copy(chunk_xnf(sc_, ci_, 4 + dc2),
                                                             B(0, dc2 * 128, 128))
                                               for dc2 in range(4)][-1],
                     reads=[("ps", 0)], writes=[("xnf", sc % 2, ci)])

    def emit_p2pair(u):
        for sli in range(NSL):
            f0 = sli * FSL
            for hh in range(2):
                srch = bass.AP(wfcT_d.tensor, f0 + hh * 512,
                               [[FF, 128], [FF * 128, 8], [1, 512]])
                S.op("gp", dma(wfc2[hh].rearrange("p (c f) -> p c f", c=8), srch),
                     writes=[("wfc", hh)])
                srcp = bass.AP(wprT_d.tensor, (f0 + hh * 512) * D,
                               [[D, 128], [D * 128, 4], [1, D]])
                S.op("gp", dma(wpr2[hh].rearrange("p (c f) -> p c f", c=4), srcp),
                     writes=[("wpr", hh)])
            for sc in (2 * u, 2 * u + 1):
                emit_p2_slice(sc, sli)

    def emit_p2_slice(sc, sli):
            xk = ([("xnf", sc % 2, i) for i in range(4)]
                  + [("xnfa", sc % 2, i) for i in range(4)])
            y_sb = y2[sc % 2]
            for fc in range(8):
                hh = fc // 4
                fcl = fc % 4
                if sli == NSL - 1:
                    mb = 7
                else:
                    mb = 0 if fc % 2 == 0 else 7
                S.op("pe", mmgroup(
                    [B(mb, 0, 512)] * 8,
                    [wfc2[hh][:, dc * 512 + fcl * 128: dc * 512 + (fcl + 1) * 128]
                     for dc in range(8)],
                    [chunk_xnf(sc, 0, dc, 0, 512) for dc in range(8)],
                    [(dc == 0, dc == 7) for dc in range(8)]),
                    reads=[("wfc", hh)] + xk, writes=[("ps", mb)])
                rl = rl2[fc % 2]
                S.op("act", lambda e, rl_=rl, mb_=mb: e.activation(rl_, B(mb_, 0, 512), AF.Relu),
                     reads=[("ps", mb)], writes=[("rl", fc % 2)])
                S.op("gp", lambda e, fc_=fc, rl_=rl: e.tensor_mul(
                    h_sb[:, fc_ * 512:(fc_ + 1) * 512], rl_, rl_),
                    reads=[("rl", fc % 2)], writes=[("h", fc)], gp_compute=True)
            for tt_ in range(4):
                j = tt_
                yb = 1 if tt_ % 2 == 0 else 3
                for fc in range(8):
                    hh = fc // 4
                    fcl = fc % 4
                    dsts, ls, rs, fl = [], [], [], []
                    for hf in range(2):
                        dsts.append(ps[:, yb * 512 + hf * 512: yb * 512 + hf * 512 + 512])
                        ls.append(h_sb[:, fc * 512 + tt_ * 128: fc * 512 + (tt_ + 1) * 128])
                        rs.append(wpr2[hh][:, fcl * D + hf * 512: fcl * D + (hf + 1) * 512])
                        fl.append((fc == 0, fc == 7))
                    S.op("pe", mmgroup(dsts, ls, rs, fl),
                         reads=[("wpr", hh), ("h", fc)],
                         writes=[("ps", yb), ("ps", yb + 1)])
                j_ = j
                if sli == 0:
                    S.op("dve", lambda e, j_=j_, yb_=yb, y_=y_sb: e.tensor_copy(
                        y_[:, j_ * D:(j_ + 1) * D], ps[:, yb_ * 512: yb_ * 512 + 1024]),
                        reads=[("ps", yb), ("ps", yb + 1)],
                        writes=[("y", sc % 2, j, 0), ("y", sc % 2, j, 1)])
                else:
                    S.op("dve", lambda e, j_=j_, yb_=yb, y_=y_sb: e.tensor_tensor(
                        y_[:, j_ * D:(j_ + 1) * D], ps[:, yb_ * 512: yb_ * 512 + 1024],
                        y_[:, j_ * D:(j_ + 1) * D], ALU.add),
                        reads=[("ps", yb), ("ps", yb + 1), ("y", sc % 2, j, 0), ("y", sc % 2, j, 1)],
                        writes=[("y", sc % 2, j, 0), ("y", sc % 2, j, 1)])

    def p3_qk(sc, p):
        ci = 2 * p
        xk = [("xnf", sc % 2, ci), ("xnf", sc % 2, ci + 1),
              ("xnfa", sc % 2, ci), ("xnfa", sc % 2, ci + 1)]
        for name, wmat, bq in (("q", wq, 1), ("k", wk, 3)):
            dsts, ls, rs, fl = [], [], [], []
            for kc in range(4):
                for dc in range(8):
                    dsts.append(ps[:, bq * 512 + kc * 256: bq * 512 + kc * 256 + 256])
                    ls.append(wmat[:, dc * KD + kc * 128: dc * KD + (kc + 1) * 128])
                    rs.append(chunk_xnf(sc, ci, dc, 0, 256))
                    fl.append((dc == 0, dc == 7))
            S.op("pe", mmgroup(dsts, ls, rs, fl),
                 reads=[("wq" if name == "q" else "wk",)] + xk,
                 writes=[("ps", bq), ("ps", bq + 1)])

    def p3_u(sc, p):
        ci = 2 * p
        xk = [("xnf", sc % 2, ci), ("xnf", sc % 2, ci + 1),
              ("xnfa", sc % 2, ci), ("xnfa", sc % 2, ci + 1)]
        S.op("pe", mmgroup(
            [ps[0:16, 0:256]] * 8,
            [wgk1[:, dc * 16:(dc + 1) * 16] for dc in range(8)],
            [chunk_xnf(sc, ci, dc, 0, 256) for dc in range(8)],
            [(dc == 0, dc == 7) for dc in range(8)]),
            reads=[("wgk1",)] + xk, writes=[("ps", 0)])
        S.op("dve", lambda e: e.tensor_copy(u_bf, ps[0:16, 0:256]),
             reads=[("ps", 0)], writes=[("u", 0), ("u", 1)])

    def p3_gates_mm(sc, ci):
        xk = [("xnf", sc % 2, ci), ("xnfa", sc % 2, ci)]
        qt, kt, kh, dcy = qt2[ci % 2], kt2[ci % 2], kh2[ci % 2], dcy2[ci % 2]
        exb, exa_ = ex2g[ci % 2]
        erb, era_ = er2g[ci % 2]
        spb, spa_ = sp2g[ci % 2]
        cb, ca_ = c2g[ci % 2]
        exk = [("exa", ci % 2)] + exa_
        erk = [("er", ci % 2)] + era_
        spk = [("sp", ci % 2)] + spa_
        ck = [("c", ci % 2)] + ca_
        hfu = (ci % 2) * 128
        S.op("pe", mmgroup(
            [B(0, kc * 128, 128) for kc in range(4)],
            [wgk2[:, kc * 128:(kc + 1) * 128] for kc in range(4)],
            [u_bf[:, hfu:hfu + 128]] * 4, [(True, True)] * 4, warm=8),
            reads=[("wgk2",), ("u", ci % 2)], writes=[("ps", 0)])
        S.op("act", lambda e, x_=exb: [e.activation(x_[:, kc * 128:(kc + 1) * 128],
                                            B(0, kc * 128, 128), AF.Exp,
                                            bias=nbgk2[:, kc:kc + 1], scale=-1.0)
                               for kc in range(4)][-1],
             reads=[("ps", 0), ("nbgk2",)], writes=exk)
        S.op("act", lambda e, x_=exb, s_=spb: e.activation(s_, x_, AF.Ln, bias=one_col, scale=1.0),
             reads=exk + [("one",)], writes=spk)
        S.op("dve", lambda e, s_=spb: e.tensor_scalar(s_, s_, 1.0 / 16.0, 3.0, ALU.mult, ALU.min),
             reads=spk, writes=spk)
        S.op("dve", lambda e, s_=spb, c_=cb: [e.tensor_tensor_scan(c_[:, kc * 128:(kc + 1) * 128],
                                                    s_[:, kc * 128:(kc + 1) * 128],
                                                    s_[:, kc * 128:(kc + 1) * 128],
                                                    0.0, ALU.add, ALU.bypass)
                               for kc in range(4)][-1],
             reads=spk, writes=ck)
        ctb = ct2g[ci % 2]
        ctv = bass.AP(cb.tensor, cb.offset + 127, [[cb.ap[0][0], 128], [128, 4]])
        S.op("dve", lambda e, ctv_=ctv, t_=ctb: e.tensor_copy(t_, ctv_), reads=ck, writes=[("ct", ci % 2)])
        S.op("act", lambda e, x_=exb, c_=cb: e.activation(x_, c_, AF.Exp, bias=0.0, scale=-1.0),
             reads=ck, writes=exk)
        S.op("act", lambda e, r_=erb, c_=cb: e.activation(r_, c_, AF.Exp, bias=0.0, scale=1.0),
             reads=ck, writes=erk)
        S.op("act", lambda e, dcy_=dcy, t_=ctb: e.activation(dcy_, t_, AF.Exp, bias=0.0, scale=-1.0),
             reads=[("ct", ci % 2)], writes=[("dcy", ci % 2)])

    def p3_gates_post(sc, ci):
        qt, kt, kh = qt2[ci % 2], kt2[ci % 2], kh2[ci % 2]
        exb, exa_ = ex2g[ci % 2]
        erb, era_ = er2g[ci % 2]
        hf = (ci % 2) * 128
        S.op("dve", lambda e, hf_=hf, qt_=qt, x_=exb: [e.tensor_tensor(
            qt_[:, kc * 128:(kc + 1) * 128],
            ps[:, 512 + kc * 256 + hf_: 512 + kc * 256 + hf_ + 128],
            x_[:, kc * 128:(kc + 1) * 128], ALU.mult) for kc in range(4)][-1],
            reads=[("ps", 1), ("ps", 2), ("exa", ci % 2)] + exa_, writes=[("qt", ci % 2)])
        S.op("dve", lambda e, hf_=hf, kt_=kt, r_=erb: [e.tensor_tensor(
            kt_[:, kc * 128:(kc + 1) * 128],
            ps[:, 3 * 512 + kc * 256 + hf_: 3 * 512 + kc * 256 + hf_ + 128],
            r_[:, kc * 128:(kc + 1) * 128], ALU.mult) for kc in range(4)][-1],
            reads=[("ps", 3), ("ps", 4), ("er", ci % 2)] + era_, writes=[("kt", ci % 2)])
        dcy_ = dcy2[ci % 2]
        S.op("dve", lambda e, kh_=kh, kt_=kt, d_=dcy_: [e.tensor_scalar(
            kh_[:, kc * 128:(kc + 1) * 128], kt_[:, kc * 128:(kc + 1) * 128],
            d_[:, kc:kc + 1], None, ALU.mult) for kc in range(4)][-1],
            reads=[("kt", ci % 2), ("dcy", ci % 2)], writes=[("kh", ci % 2)])

    def p3_vw(sc, ci):
        xk = [("xnf", sc % 2, ci), ("xnfa", sc % 2, ci)]
        xwg_bf = xwg2[ci % 2]
        for name, wmat, buf, bkey in (("v", wv, v_bf, ("v",)), ("w", wg, xwg_bf, ("xwg", ci % 2))):
            dsts, ls, rs, fl = [], [], [], []
            for hf in range(2):
                for dc in range(8):
                    dsts.append(B(5, 0, 1024)[:, hf * 512:(hf + 1) * 512])
                    ls.append(chunk_xnf(sc, ci, dc))
                    rs.append(wmat[:, dc * VD + hf * 512: dc * VD + (hf + 1) * 512])
                    fl.append((dc == 0, dc == 7))
            S.op("pe", mmgroup(dsts, ls, rs, fl),
                 reads=[("wv" if name == "v" else "wg",)] + xk, writes=[("ps", 5), ("ps", 6)])
            if name == "v":
                S.op("dve", lambda e, buf_=buf: e.tensor_copy(buf_, B(5, 0, 1024)),
                     reads=[("ps", 5), ("ps", 6)], writes=[bkey])
            else:
                S.op("act", lambda e, buf_=buf: e.activation(buf_, B(5, 0, 1024), AF.Copy),
                     reads=[("ps", 5), ("ps", 6)], writes=[bkey])

    def p3_mid(sc, ci):
        qt, kt, kh, dcy = qt2[ci % 2], kt2[ci % 2], kh2[ci % 2], dcy2[ci % 2]
        pt = kt
        S.op("pe", mmgroup(
            [B(7, h * 128, 128) for h in range(H)],
            [kh[:, h * 128:(h + 1) * 128] for h in range(H)],
            [ident] * H, [(True, True)] * H, warm=8),
            reads=[("kh", ci % 2), ("ident",)], writes=[("ps", 7)])
        S.op("dve", lambda e: e.tensor_copy(khT, B(7, 0, 512)),
             reads=[("ps", 7)], writes=[("khT",)])
        S.op("pe", mmgroup(
            [B(7, h * 128, 128) for h in range(H)],
            [kt[:, h * 128:(h + 1) * 128] for h in range(H)],
            [qt[:, h * 128:(h + 1) * 128] for h in range(H)],
            [(True, True)] * H, warm=8),
            reads=[("kt", ci % 2), ("qt", ci % 2)], writes=[("ps", 7)])
        S.op("dve", lambda e: [e.tensor_tensor(pt[:, h * 128:(h + 1) * 128],
                                               B(7, h * 128, 128), tri, ALU.mult)
                               for h in range(H)][-1],
             reads=[("ps", 7), ("tri",)], writes=[("kt", ci % 2)])
        dsts, ls, rs, fl = [], [], [], []
        for h in range(H):
            dsts += [B(5, 0, 1024)[:, h * 256:(h + 1) * 256]] * 2
            ls += [pt[:, h * 128:(h + 1) * 128], qt[:, h * 128:(h + 1) * 128]]
            rs += [v_bf[:, h * 256:(h + 1) * 256], S_bf[:, h * 256:(h + 1) * 256]]
            fl += [(True, False), (False, True)]
        S.op("pe", mmgroup(dsts, ls, rs, fl, warm=8),
             reads=[("kt", ci % 2), ("qt", ci % 2), ("v",), ("S",)], writes=[("ps", 5), ("ps", 6)])
        S.op("dve", lambda e: e.tensor_copy(o_bf, B(5, 0, 1024)),
             reads=[("ps", 5), ("ps", 6)], writes=[("o",)])
        S.op("pe", mmgroup(
            [B(5, 0, 1024)[:, h * 256:(h + 1) * 256] for h in range(H)],
            [khT[:, h * 128:(h + 1) * 128] for h in range(H)],
            [v_bf[:, h * 256:(h + 1) * 256] for h in range(H)],
            [(True, True)] * H, warm=8),
            reads=[("khT",), ("v",)], writes=[("ps", 5), ("ps", 6)])
        S.op("dve", lambda e, dcy_=dcy: [e.scalar_tensor_tensor(S_bf[:, h * 256:(h + 1) * 256],
                                                      S_bf[:, h * 256:(h + 1) * 256],
                                                      dcy_[:, h:h + 1],
                                                      B(5, 0, 1024)[:, h * 256:(h + 1) * 256],
                                                      ALU.mult, ALU.add)
                               for h in range(H)][-1],
             reads=[("ps", 5), ("ps", 6), ("dcy", ci % 2), ("S",)], writes=[("S",)])

    def p3_silu(sc, ci):
        xwg_bf = xwg2[ci % 2]
        xwk = ("xwg", ci % 2)
        for hh in range(4):
            S.op("act", lambda e, hh_=hh, x_=xwg_bf: e.activation(
                ev, x_[:, hh_ * 256:(hh_ + 1) * 256], AF.Exp),
                reads=[xwk], writes=[("ev",)])
            S.op("act", lambda e: e.activation(sl, ev, AF.Ln, bias=one_col, scale=1.0),
                 reads=[("ev",), ("one",)], writes=[("sl",)])
            S.op("dve", lambda e, hh_=hh, x_=xwg_bf: e.tensor_sub(
                ev, x_[:, hh_ * 256:(hh_ + 1) * 256], sl),
                reads=[xwk, ("sl",)], writes=[("ev",)])
            S.op("act", lambda e, hh_=hh: e.activation(sg[:, hh_ * 256:(hh_ + 1) * 256],
                                                       ev, AF.Exp),
                 reads=[("ev",)], writes=[("sg", hh % 2)])

    def p3_epi_pre(sc, ci):
        xwg_bf = xwg2[ci % 2]
        xwk = ("xwg", ci % 2)
        S.op("act", lambda e: [e.activation(og[:, h * 256:(h + 1) * 256],
                                            o_bf[:, h * 256:(h + 1) * 256],
                                            AF.Square, accum_out=msq[:, h:h + 1])
                               for h in range(H)][-1],
             reads=[("o",)], writes=[("msq",), ("og",)])
        S.op("act", lambda e: e.activation(rfac, msq, AF.Ln, bias=eps_col, scale=1.0 / DV),
             reads=[("msq",), ("eps",)], writes=[("rfac",)])
        S.op("act", lambda e: e.activation(rfac, rfac, AF.Exp, bias=0.0, scale=-0.5),
             reads=[("rfac",)], writes=[("rfac",)])
        S.op("dve", lambda e: e.tensor_tensor(sg, sg, rmsw, ALU.mult),
             reads=[("sg", 0), ("sg", 1), ("rmsw",)], writes=[("sg", 0), ("sg", 1)])
        S.op("dve", lambda e, x_=xwg_bf: e.tensor_tensor(sg, sg, x_, ALU.mult),
             reads=[("sg", 0), ("sg", 1), xwk], writes=[("sg", 0), ("sg", 1)])
        S.op("dve", lambda e: [e.scalar_tensor_tensor(
            og[:, h * 256:(h + 1) * 256],
            o_bf[:, h * 256:(h + 1) * 256], rfac[:, h:h + 1],
            sg[:, h * 256:(h + 1) * 256], ALU.mult, ALU.mult)
            for h in range(H)][-1],
            reads=[("o",), ("rfac",), ("sg", 0), ("sg", 1)], writes=[("og",)])

    def p3_epi_ogt(sc, ci):
        for w in range(2):
            S.op("pe", mmgroup(
                [B(7, vc2 * 128, 128) for vc2 in range(4)],
                [og[:, (w * 4 + vc2) * 128:(w * 4 + vc2 + 1) * 128] for vc2 in range(4)],
                [ident] * 4, [(True, True)] * 4, warm=8),
                reads=[("og",), ("ident",)], writes=[("ps", 7)])
            if w == 0:
                S.op("act", lambda e: e.activation(ogf[:, 0:512], B(7, 0, 512), AF.Copy),
                     reads=[("ps", 7)], writes=[("ogfa",)])
            else:
                S.op("dve", lambda e: e.tensor_copy(ogf[:, 512:1024], B(7, 0, 512)),
                     reads=[("ps", 7)], writes=[("ogf",)])

    def p3_spike(sc, ci):
        yk = [("y", sc % 2, ci, 0), ("y", sc % 2, ci, 1)]
        yi = y2[sc % 2][:, ci * D:(ci + 1) * D]
        gb = g2[ci % 2]
        for j in range(5):
            sjb = sj2[0]
            S.op("act", lambda e, j_=j, s_=sjb, yi_=yi: e.activation(
                s_, yi_, AF.Sign, bias=th_col[:, j_:j_ + 1], scale=1.0),
                reads=yk + [("th",)], writes=[("sj", 0)])
            if j == 0:
                S.op("dve", lambda e, s_=sjb, gb_=gb: e.tensor_scalar(gb_, s_, DS2[0], C0, ALU.mult, ALU.add),
                     reads=[("sj", 0)], writes=[("g", ci % 2)])
            else:
                S.op("dve", lambda e, j_=j, s_=sjb, gb_=gb: e.scalar_tensor_tensor(
                    gb_, s_, DS2[j_], gb_, ALU.mult, ALU.add),
                    reads=[("sj", 0), ("g", ci % 2)], writes=[("g", ci % 2)])

    def p3_tail(sc, ci):
        ch = sc * CPS + ci
        t0 = ch * C
        bq = 1 if ci % 2 == 0 else 3
        for half in range(2):
            dsts, ls, rs, fl = [], [], [], []
            for hf in range(2):
                for vc in range(half * 4, half * 4 + 4):
                    dsts.append(ps[:, bq * 512 + hf * 512: bq * 512 + hf * 512 + 512])
                    ls.append(ogf[:, vc * 128:(vc + 1) * 128])
                    rs.append(wout[:, vc * D + hf * 512: vc * D + (hf + 1) * 512])
                    fl.append((vc == half * 4 and half == 0, vc == half * 4 + 3 and half == 1))
            S.op("pe", mmgroup(dsts, ls, rs, fl),
                 reads=[("ogfa",) if half == 0 else ("ogf",), ("wout",)],
                 writes=[("ps", bq), ("ps", bq + 1)])
        yslot = y2[sc % 2][:, ci * D:(ci + 1) * D]
        S.op("sync", dma(yslot, xd[t0:t0 + C, :]),
             writes=[("y", sc % 2, ci, 0), ("y", sc % 2, ci, 1)])
        gb = g2[ci % 2]
        for hh in range(2):
            yk1 = ("y", sc % 2, ci, hh)
            S.op("dve", lambda e, bq_=bq, gb_=gb, hh_=hh: e.scalar_tensor_tensor(
                ob, ps[:, bq_ * 512 + hh_ * 512: bq_ * 512 + hh_ * 512 + 512], 0.0,
                gb_[:, hh_ * 512:(hh_ + 1) * 512], ALU.bypass, ALU.mult),
                reads=[("ps", bq + hh), ("g", ci % 2)], writes=[("ob",)])
            S.op("dve", lambda e, y_=yslot, hh_=hh: e.tensor_tensor(
                y_[:, hh_ * 512:(hh_ + 1) * 512], ob,
                y_[:, hh_ * 512:(hh_ + 1) * 512], ALU.add),
                reads=[("ob",), yk1], writes=[yk1])
            S.op("sync", lambda e, t0_=t0, hh_=hh, y_=yslot: e.dma_start(
                outd[t0_:t0_ + C, hh_ * 512:(hh_ + 1) * 512], y_[:, hh_ * 512:(hh_ + 1) * 512]),
                reads=[yk1], writes=[("outdma", ch, hh)], out_dma=True)

    def emit_p3(sc):
        p3_u(sc, 0)
        p3_gates_mm(sc, 0)
        p3_vw(sc, 0)
        for p in range(CPS // 2):
            A, Bc = 2 * p, 2 * p + 1
            p3_qk(sc, p)
            p3_gates_post(sc, A)
            p3_gates_mm(sc, Bc)
            p3_mid(sc, A)
            p3_gates_post(sc, Bc)
            p3_silu(sc, A)
            p3_epi_pre(sc, A)
            p3_vw(sc, Bc)
            p3_spike(sc, A)
            p3_mid(sc, Bc)
            p3_silu(sc, Bc)
            p3_epi_ogt(sc, A)
            p3_epi_pre(sc, Bc)
            p3_tail(sc, A)
            if p < CPS // 2 - 1:
                p3_u(sc, p + 1)
                p3_gates_mm(sc, 2 * p + 2)
                p3_vw(sc, 2 * p + 2)
            p3_spike(sc, Bc)
            p3_epi_ogt(sc, Bc)
            p3_tail(sc, Bc)

    for u in range(NSUP // 2):
        for sc in (2 * u, 2 * u + 1):
            for ci in range(CPS):
                emit_p1(sc, ci)
        emit_p2pair(u)
        emit_p3(2 * u)
        emit_p3(2 * u + 1)

    for s in Sched.DMA_ROT["out"]:
        S.raw("sync", "wait", s, 0)

    import os as _os
    prog = S.emit(nc, sems, reorder=_os.environ.get("K_NO_REORDER") != "1")
    with nc.Block() as block:
        def runner(eng_name):
            def run(e):
                for item in prog[eng_name]:
                    if item[0] == "wait":
                        e.wait_ge(sems[item[1]], item[2])
                    elif item[0] == "drain":
                        e.drain()
                    else:
                        _, fn, sem, inc = item
                        inst = fn(e)
                        if isinstance(inst, list):
                            inst = inst[-1]
                        if inc:
                            inst.then_inc(sems[sem], inc)
            return run
        block.tensor(runner("pe"))
        block.vector(runner("dve"))
        block.scalar(runner("act"))
        block.sync(runner("sync"))
        block.gpsimd(runner("gp"))

    es.close()
    return nc


_NC_CACHE = None


def kernel(L0_attn_x, L0_attn_Wq, L0_attn_Wk, L0_attn_Wv, L0_attn_Wg,
           L0_attn_Wgk1, L0_attn_Wgk2, L0_attn_bgk2, L0_attn_Wout,
           L0_attn_rms_w, L0_ffn_W_fc, L0_ffn_W_proj):
    global _NC_CACHE
    n = 8
    x = np.ascontiguousarray(np.asarray(L0_attn_x, dtype=np.float32))
    f32 = lambda a: np.ascontiguousarray(np.asarray(a, dtype=np.float32))
    com = {
        "wq": f32(L0_attn_Wq), "wk": f32(L0_attn_Wk), "wv": f32(L0_attn_Wv),
        "wg": f32(L0_attn_Wg), "wgk1": f32(L0_attn_Wgk1), "wgk2": f32(L0_attn_Wgk2),
        "bgk2": np.ascontiguousarray(f32(L0_attn_bgk2).reshape(4, 128).T),
        "wout": f32(L0_attn_Wout), "rmsw": f32(L0_attn_rms_w),
        "wfcT": np.ascontiguousarray(f32(L0_ffn_W_fc).T),
        "wprT": np.ascontiguousarray(f32(L0_ffn_W_proj).T),
        "ident": np.eye(128, dtype=ml_dtypes.bfloat16),
        "tri": np.triu(np.ones((128, 128))).astype(ml_dtypes.bfloat16),
    }
    if _NC_CACHE is None:
        _NC_CACHE = build_nc()
    nc = _NC_CACHE
    in_maps = [dict(com, x=np.ascontiguousarray(x[b])) for b in range(n)]
    res = run_bass_kernel_spmd(nc, in_maps, core_ids=list(range(n)))
    return np.stack([np.asarray(r["out"], dtype=np.float32) for r in res.results], axis=0)
